# version 23
# baseline (speedup 1.0000x reference)
import numpy as np
import ml_dtypes
from contextlib import ExitStack

import concourse.bass as bass
import concourse.mybir as mybir
from concourse.bass_utils import run_bass_kernel_spmd

F32 = mybir.dt.float32
BF16 = mybir.dt.bfloat16
I32 = mybir.dt.int32
AF = mybir.ActivationFunctionType
ALU = mybir.AluOpType
AX = mybir.AxisListType

B, T, D = 4, 4096, 1024
FF = 4096
NC = 8
TQ = 2048
SLOT_P = [8, 16, 24, 32]
QCH = {0: [0, 3, 4, 7], 1: [1, 2, 5, 6]}
EPS_Q = 1e-8
EPS_LN = 1e-5
NEG = -1e30


def groups_of(P):
    G = (P + 2) // 3
    sizes = [3] * (P - 2 * G) + [2] * (3 * G - P)
    out, s = [], 0
    for sz in sizes:
        out.append(list(range(s, s + sz)))
        s += sz
    assert s == P
    return out


GRPS = [groups_of(P) for P in SLOT_P]
NG = [len(g) for g in GRPS]
TSC_CUM = [0]
for _k in range(4):
    TSC_CUM.append(TSC_CUM[-1] + 8 * NG[_k])


def _mask_width(k, r):
    P = SLOT_P[k]
    c = P - 8 + r
    w = 0
    for parity in (0, 1):
        cq = QCH[parity][k]
        E = 4 * (cq + 1)
        if c >= E:
            w = 512
        else:
            d = 512 * cq - 128 * c
            w = max(w, max(0, min(512, 128 - d)))
    return w



MW = [[_mask_width(k, r) for r in range(8)] for k in range(4)]


class _NullEng:
    def __call__(self, f):
        return None


class _NullBlockInner:
    def __getattr__(self, name):
        return _NullEng()


class _NullBlock:
    def __enter__(self):
        return _NullBlockInner()

    def __exit__(self, *a):
        return False


class Sem:
    def __init__(self, ctx, nc, name):
        self.h = ctx.enter_context(nc.semaphore(name))
        self.n = 0


PHASES = 6


def build_program(debug=False, phases=None):
    phases = PHASES if phases is None else phases
    nc = bass.Bass("TRN2", target_bir_lowering=False)

    x_kv = nc.dram_tensor("x_kv", [T, D], F32, kind="ExternalInput")
    x_own = nc.dram_tensor("x_own", [TQ, D], F32, kind="ExternalInput")
    wqkvT = nc.dram_tensor("wqkvT", [D, 3 * D], F32, kind="ExternalInput")
    woutT = nc.dram_tensor("woutT", [D, D], F32, kind="ExternalInput")
    w1 = nc.dram_tensor("w1", [FF, D], F32, kind="ExternalInput")
    b1 = nc.dram_tensor("b1", [128, 32], F32, kind="ExternalInput")
    w2 = nc.dram_tensor("w2", [D, FF], F32, kind="ExternalInput")
    b2 = nc.dram_tensor("b2", [128, 8], F32, kind="ExternalInput")
    masks = nc.dram_tensor("masks", [4, 8, 128, 512], BF16, kind="ExternalInput")
    ident_bf = nc.dram_tensor("ident_bf", [128, 128], BF16, kind="ExternalInput")
    out = nc.dram_tensor("out", [TQ, D], F32, kind="ExternalOutput")

    kt_d = nc.dram_tensor("kt_d", [8, 128, T], BF16)
    v_d = nc.dram_tensor("v_d", [32, 128, D], BF16)
    w1qt_d = nc.dram_tensor("w1qt_d", [32, 128, 8, 128], BF16)
    w2qt_d = nc.dram_tensor("w2qt_d", [32, 128, D], BF16)
    h_d = nc.dram_tensor("h_d", [32, 128, TQ], BF16)
    x2_d = nc.dram_tensor("x2_d", [TQ, D], BF16)
    cc_in = nc.dram_tensor("cc_in", [1, 8], F32)
    cc_out = nc.dram_tensor("cc_out", [1, 8], F32)
    bc_d = nc.dram_tensor("bc_d", [1, 8], F32)
    b2row_d = nc.dram_tensor("b2row_d", [1, D], F32)

    dbg = {}
    if debug:
        for nm, shp in [("qT", [128, 8 * TQ]), ("x2", [128, 16 * D]),
                        ("xqT", [128, 8 * TQ]), ("s12", [1, 8]),
                        ("kt0", [128, T]), ("v0", [128, D]), ("h0", [128, TQ]),
                        ("hstat", [128, 128]), ("b1q", [128, 32]), ("b2qb", [128, D]),
                        ("sw1", [128, 32]), ("sw2", [128, 8])]:
            dbg[nm] = nc.dram_tensor("dbg_" + nm, shp, F32, kind="ExternalOutput")

    ctx = ExitStack()
    with ctx:
        E = ctx.enter_context
        qT = E(nc.sbuf_tensor("qT", [128, 8 * TQ], BF16))
        x2T = E(nc.sbuf_tensor("x2T", [128, 8, TQ], BF16))
        sw1 = E(nc.sbuf_tensor("sw1", [128, 32], F32))
        sw2 = E(nc.sbuf_tensor("sw2", [128, 8], F32))
        b1s = E(nc.sbuf_tensor("b1s", [128, 32], F32))
        b2s = E(nc.sbuf_tensor("b2s", [128, 8], F32))
        b1qs = E(nc.sbuf_tensor("b1qs", [128, 32], F32))
        b2qs = E(nc.sbuf_tensor("b2qs", [128, 8], F32))
        x2max = E(nc.sbuf_tensor("x2max", [128, 16], F32))
        hstat = E(nc.sbuf_tensor("hstat", [128, 128], F32))
        onesb = E(nc.sbuf_tensor("onesb", [128, 64], BF16))
        identb = E(nc.sbuf_tensor("identb", [128, 128], BF16))
        scal8 = E(nc.sbuf_tensor("scal8", [128, 8], F32))
        tiny = E(nc.sbuf_tensor("tiny", [1, 8], F32))
        tiny2 = E(nc.sbuf_tensor("tiny2", [1, 8], F32))
        b2qb = E(nc.sbuf_tensor("b2qb", [128, D], F32))
        red1 = E(nc.sbuf_tensor("red1", [128, 4], F32))
        epsln = E(nc.sbuf_tensor("epsln", [128, 1], F32))
        pw2 = ExitStack()
        woutT_s = pw2.enter_context(nc.sbuf_tensor("woutT_s", [128, 8 * D], BF16))
        mskb = [pw2.enter_context(nc.sbuf_tensor(f"mskb{i}", [128, 8 * 512], BF16)) for i in range(2)]

        S = {}
        for nm in ["ldx", "trx", "ldo", "tro", "ldc", "mm", "ev", "sp",
                   "ktl", "vl", "mkl", "xrl", "tsc", "exp", "pv", "rcp", "div",
                   "op", "res", "lnr", "xt2",
                   "wl", "qv", "wtr", "wsp", "msk",
                   "gp", "cc", "v3", "xql", "xqt",
                   "w1l", "f1", "hev", "hsp", "v5",
                   "hl", "hq", "w2l", "f2", "y2e", "ln2", "ot", "o", "x2l"]:
            S[nm] = Sem(ctx, nc, "s_" + nm)

        def _tail_dump():
            if not debug:
                return
            with nc.Block() as blk:
                @blk.gpsimd
                def _(g):
                    pairs = [("kt0", kt_d.ap()[0, :, :]), ("v0", v_d.ap()[0, :, :]),
                             ("qT", qT[:, :])]
                    pass
                    if phases >= 3:
                        pairs.append(("s12", bc_d.ap()))
                        pairs.append(("b1q", b1qs[:, :]))
                    if phases >= 4:
                        pairs.append(("h0", h_d.ap()[0, :, :]))
                        pairs.append(("hstat", hstat[:, :]))
                    for nm, srcp in pairs:
                        g.dma_start(out=dbg[nm].ap(), in_=srcp).then_inc(S["gp"].h, 16); S["gp"].n += 16
                    g.wait_ge(S["gp"].h, S["gp"].n)

        wqst = {"mload": 0, "nwq": 0, "mq": 0}

        def wload_src(m):
            if m < 32:
                return w1.ap()[128 * m:128 * (m + 1), :]
            q = m - 32
            jj, qq = q // 4, q % 4
            return w2.ap()[128 * jj:128 * (jj + 1), 1024 * qq:1024 * (qq + 1)]

        def emit_wload(eng):
            m = wqst["mload"]
            if m >= 4:
                eng.wait_ge(S["qv"].h, m - 3)
            eng.dma_start(out=wstg[m % 4][:, :], in_=wload_src(m)).then_inc(S["wl"].h, 16); S["wl"].n += 16
            wqst["mload"] = m + 1

        def emit_wq_step(sy):
            nwq = wqst["nwq"]
            sy.wait_ge(S["qv"].h, nwq + 1)
            if nwq >= 2:
                sy.wait_ge(S["wsp"].h, 16 * (nwq - 1))
            sy.dma_start_transpose(wtstg[nwq % 2][:, :, :],
                                   wqstg[nwq % 4][:, :]).then_inc(S["wtr"].h, 16); S["wtr"].n += 16
            sy.wait_ge(S["wtr"].h, S["wtr"].n)
            if nwq < 32:
                dst = w1qt_d.ap()[nwq, :, :, :]
            else:
                q = nwq - 32
                jj, qq = q // 4, q % 4
                dst = w2qt_d.ap()[8 * qq:8 * qq + 8, :, 128 * jj:128 * (jj + 1)].rearrange("a p f -> p a f")
            sy.dma_start(out=dst, in_=wtstg[nwq % 2][:, :, :]).then_inc(S["wsp"].h, 16); S["wsp"].n += 16
            wqst["nwq"] = nwq + 1

        def emit_w1_quant(v):
            j = wqst["mq"]
            v.wait_ge(S["wl"].h, 16 * (j + 1))
            if j >= 4:
                v.wait_ge(S["wtr"].h, 16 * (j - 3))
            src_t = wstg[j % 4]
            v.drain()
            v.reduce_max(red1[:, 0:1], src_t[:, :], axis=AX.X, apply_absolute_value=True)
            v.drain()
            v.tensor_scalar(sw1[:, j:j + 1], red1[:, 0:1], EPS_Q, 1.0 / 127.0, ALU.max, ALU.mult)
            v.drain()
            v.reciprocal(red1[:, 1:2], sw1[:, j:j + 1])
            v.drain()
            v.tensor_scalar(src_t[:, :], src_t[:, :], red1[:, 1:2], None, ALU.mult)
            v.tensor_copy(wint[:, :], src_t[:, :])
            v.tensor_scalar(wqstg[j % 4][:, :], wint[:, :], sw1[:, j:j + 1], None,
                            ALU.mult).then_inc(S["qv"].h, 1); S["qv"].n += 1
            wqst["mq"] = j + 1

        def emit_w2_group(v):
            q = wqst["mq"] - 32
            jj = q // 4
            v.wait_ge(S["wl"].h, 16 * (32 + 4 * jj + 4))
            if jj >= 1:
                v.wait_ge(S["wtr"].h, 16 * (32 + 4 * (jj - 1) + 4))
            v.drain()
            for qq in range(4):
                v.reduce_max(red1[:, qq:qq + 1], wstg[qq][:, :], axis=AX.X, apply_absolute_value=True)
            v.drain()
            v.tensor_tensor(red1[:, 0:1], red1[:, 0:1], red1[:, 1:2], ALU.max)
            v.tensor_tensor(red1[:, 2:3], red1[:, 2:3], red1[:, 3:4], ALU.max)
            v.drain()
            v.tensor_tensor(red1[:, 0:1], red1[:, 0:1], red1[:, 2:3], ALU.max)
            v.drain()
            v.tensor_scalar(sw2[:, jj:jj + 1], red1[:, 0:1], EPS_Q, 1.0 / 127.0, ALU.max, ALU.mult)
            v.drain()
            v.reciprocal(red1[:, 1:2], sw2[:, jj:jj + 1])
            v.drain()
            for qq in range(4):
                v.tensor_scalar(wstg[qq][:, :], wstg[qq][:, :], red1[:, 1:2], None, ALU.mult)
                v.tensor_copy(wint[:, :], wstg[qq][:, :])
                v.tensor_scalar(wqstg[(32 + 4 * jj + qq) % 4][:, :], wint[:, :], sw2[:, jj:jj + 1], None,
                                ALU.mult).then_inc(S["qv"].h, 1); S["qv"].n += 1
            wqst["mq"] += 4

        with ExitStack() as p1:
            E1 = p1.enter_context
            wqkvT_s = E1(nc.sbuf_tensor("wqkvT_s", [128, 8 * 3 * D], BF16))
            xtw = [E1(nc.sbuf_tensor(f"xtw{i}", [128, 8, 512], BF16)) for i in range(2)]
            xow = [E1(nc.sbuf_tensor(f"xow{i}", [128, 8, 512], BF16)) for i in range(3)]
            xstg = [E1(nc.sbuf_tensor(f"xstg{i}", [128, D], BF16)) for i in range(2)]
            xostg = [E1(nc.sbuf_tensor(f"xostg{i}", [128, D], BF16)) for i in range(2)]
            kvstg = [E1(nc.sbuf_tensor(f"kvstg{i}", [128, 512], BF16)) for i in range(8)]
            ps = [E1(nc.psum_tensor(f"p1ps{i}", [128, 512], F32)) for i in range(4)]

            with (nc.Block() if phases >= 1 else _NullBlock()) as blk:
                @blk.gpsimd
                def _(g):
                    g.dma_start(out=identb[:, :], in_=ident_bf.ap()).then_inc(S["ldc"].h, 16); S["ldc"].n += 16
                    g.dma_start(out=wqkvT_s[:, :].rearrange("p (a f) -> p a f", a=8),
                                in_=wqkvT.ap().rearrange("(a p) f -> p a f", p=128)).then_inc(S["ldc"].h, 16); S["ldc"].n += 16

                    def spills_for_window(w):
                        base = 16 * w
                        for u in range(16):
                            nsp = base + u + 1
                            g.wait_ge(S["ev"].h, nsp)
                            if u < 8:
                                g.dma_start(out=kt_d.ap()[u, :, 512 * w:512 * (w + 1)],
                                            in_=kvstg[(nsp - 1) % 8][:, :]).then_inc(S["sp"].h, 16)
                            else:
                                i, fh = (u - 8) // 2, (u - 8) % 2
                                g.dma_start(out=v_d.ap()[4 * w + i, :, 512 * fh:512 * (fh + 1)],
                                            in_=kvstg[(nsp - 1) % 8][:, :]).then_inc(S["sp"].h, 16)
                            S["sp"].n += 16

                    io = 0
                    for i in range(32):
                        if i >= 2:
                            g.wait_ge(S["trx"].h, 16 * (i - 1))
                        g.dma_start(out=xstg[i % 2][:, :],
                                    in_=x_kv.ap()[128 * i:128 * (i + 1), :]).then_inc(S["ldx"].h, 16); S["ldx"].n += 16
                        if i % 2 == 1 and io < 14:
                            if io >= 2:
                                g.wait_ge(S["tro"].h, 16 * (io - 1))
                            g.dma_start(out=xostg[io % 2][:, :],
                                        in_=x_own.ap()[128 * io:128 * (io + 1), :]).then_inc(S["ldo"].h, 16); S["ldo"].n += 16
                            io += 1
                        if i % 4 == 3 and i // 4 >= 1:
                            spills_for_window(i // 4 - 1)
                    spills_for_window(7)
                    g.dma_start(out=woutT_s[:, :].rearrange("p (a f) -> p a f", a=8),
                                in_=woutT.ap().rearrange("(a p) f -> p a f", p=128)).then_inc(S["ldc"].h, 16); S["ldc"].n += 16
                    g.dma_start(out=mskb[0][:, :].rearrange("p (a f) -> p a f", a=8),
                                in_=masks.ap()[0, :, :, :].rearrange("a p f -> p a f")).then_inc(S["mkl"].h, 16); S["mkl"].n += 16
                    for io in range(14, 16):
                        g.wait_ge(S["tro"].h, 16 * (io - 1))
                        g.dma_start(out=xostg[io % 2][:, :],
                                    in_=x_own.ap()[128 * io:128 * (io + 1), :]).then_inc(S["ldo"].h, 16); S["ldo"].n += 16

                @blk.sync
                def _(sy):
                    io = 0
                    for i in range(32):
                        sy.wait_ge(S["ldx"].h, 16 * (i + 1))
                        w, sl = i // 4, i % 4
                        if sl == 0 and w >= 2:
                            sy.wait_ge(S["mm"].h, 16 * (w - 1))
                        sy.dma_start_transpose(xtw[w % 2][:, :, 128 * sl:128 * (sl + 1)],
                                               xstg[i % 2][:, :]).then_inc(S["trx"].h, 16); S["trx"].n += 16
                        if i % 2 == 1 and io < 12:
                            sy.wait_ge(S["ldo"].h, 16 * (io + 1))
                            w2_, sl2 = io // 4, io % 4
                            sy.dma_start_transpose(xow[w2_ % 3][:, :, 128 * sl2:128 * (sl2 + 1)],
                                                   xostg[io % 2][:, :]).then_inc(S["tro"].h, 16); S["tro"].n += 16
                            io += 1
                    sy.wait_ge(S["mm"].h, 136)
                    for io in range(12, 16):
                        sy.wait_ge(S["ldo"].h, 16 * (io + 1))
                        sy.dma_start_transpose(xow[0][:, :, 128 * (io % 4):128 * (io % 4 + 1)],
                                               xostg[io % 2][:, :]).then_inc(S["tro"].h, 16); S["tro"].n += 16
                    sy.dma_start(out=b1s[:, :], in_=b1.ap()).then_inc(S["ldc"].h, 16); S["ldc"].n += 16
                    sy.dma_start(out=b2s[:, :], in_=b2.ap()).then_inc(S["ldc"].h, 16); S["ldc"].n += 16

                @blk.scalar
                def _(s):
                    nev = 0
                    for w in range(8):
                        for u in range(16):
                            nev += 1
                            s.wait_ge(S["mm"].h, nev)
                            if nev > 8:
                                s.wait_ge(S["sp"].h, 16 * (nev - 8))
                            s.copy(kvstg[(nev - 1) % 8][:, :], ps[(nev - 1) % 4][:, :]).then_inc(S["ev"].h, 1); S["ev"].n += 1
                    for w in range(4):
                        for f in range(8):
                            nev += 1
                            s.wait_ge(S["mm"].h, nev)
                            s.copy(qT[:, f * TQ + 512 * w: f * TQ + 512 * (w + 1)],
                                   ps[(nev - 1) % 4][:, :]).then_inc(S["ev"].h, 1); S["ev"].n += 1

                @blk.tensor
                def _(t):
                    t.wait_ge(S["ldc"].h, 32)
                    nmm = 0
                    for w in range(8):
                        t.wait_ge(S["trx"].h, 16 * 4 * (w + 1))
                        xw = xtw[w % 2]
                        for u in range(16):
                            nmm += 1
                            if nmm > 4:
                                t.wait_ge(S["ev"].h, nmm - 4)
                            for c in range(8):
                                last = (c == 7)
                                if u < 8:
                                    mm = t.matmul(ps[(nmm - 1) % 4][:, :],
                                                  wqkvT_s[:, c * 3 * D + D + 128 * u: c * 3 * D + D + 128 * (u + 1)],
                                                  xw[:, c:c + 1, :],
                                                  start=(c == 0), stop=last)
                                else:
                                    i, fh = (u - 8) // 2, (u - 8) % 2
                                    mm = t.matmul(ps[(nmm - 1) % 4][:, :],
                                                  xw[:, c:c + 1, 128 * i:128 * (i + 1)],
                                                  wqkvT_s[:, c * 3 * D + 2 * D + 512 * fh: c * 3 * D + 2 * D + 512 * (fh + 1)],
                                                  start=(c == 0), stop=last)
                                if last:
                                    mm.then_inc(S["mm"].h, 1); S["mm"].n += 1
                    for w in range(4):
                        t.wait_ge(S["tro"].h, 16 * 4 * (w + 1))
                        xw = xow[w % 3]
                        for f in range(8):
                            nmm += 1
                            if nmm > 4:
                                t.wait_ge(S["ev"].h, nmm - 4)
                            for c in range(8):
                                mm = t.matmul(ps[(nmm - 1) % 4][:, :],
                                              wqkvT_s[:, c * 3 * D + 128 * f: c * 3 * D + 128 * (f + 1)],
                                              xw[:, c:c + 1, :],
                                              start=(c == 0), stop=(c == 7))
                                if c == 7:
                                    mm.then_inc(S["mm"].h, 1); S["mm"].n += 1

                @blk.vector
                def _(v):
                    v.memset(onesb[:, :], 1.0)
                    v.memset(epsln[:, :], EPS_LN)
                    v.memset(x2max[:, :], NEG)
                    v.memset(tiny[:, :], NEG)
                    v.memset(tiny2[:, :], NEG)

        with ExitStack() as p2:
            E2 = p2.enter_context
            ktw = [E2(nc.sbuf_tensor(f"ktw{i}", [128, 32 * 128], BF16)) for i in range(2)]
            vw = [E2(nc.sbuf_tensor(f"vw{i}", [128, 32 * 128], BF16)) for i in range(2)]
            wstg = [E2(nc.sbuf_tensor(f"wstg{i}", [128, 1024], F32)) for i in range(4)]
            wqstg = [E2(nc.sbuf_tensor(f"wqstg{i}", [128, 1024], BF16)) for i in range(4)]
            wtstg = [E2(nc.sbuf_tensor(f"wtstg{i}", [128, 8, 128], BF16)) for i in range(2)]
            wint = E2(nc.sbuf_tensor("wint", [128, 1024], I32))
            expb = [[E2(nc.sbuf_tensor(f"expb{i}{hh}", [128, 1536], BF16)) for hh in range(2)] for i in range(2)]
            outT = E2(nc.sbuf_tensor("outT", [128, 8 * 512], BF16))
            rcpa = E2(nc.sbuf_tensor("rcpa", [128, 512], F32))
            rcpb = E2(nc.sbuf_tensor("rcpb", [128, 512], F32))
            xres = E2(nc.sbuf_tensor("xres", [128, D], F32))
            xr = [E2(nc.sbuf_tensor(f"xr{i}", [128, D], F32)) for i in range(2)]
            x2stg = [E2(nc.sbuf_tensor(f"x2stg{i}", [128, D], BF16)) for i in range(2)]
            bns = E2(nc.sbuf_tensor("bns", [128, 12], F32))
            mv = [E2(nc.sbuf_tensor(f"mv{i}", [128, 4], F32)) for i in range(2)]
            psA = E2(nc.psum_tensor("psA", [128, 1536], F32))
            psB = E2(nc.psum_tensor("psB", [128, 1536], F32))
            pvp = E2(nc.psum_tensor("pvp", [128, 512], F32))
            dnp = E2(nc.psum_tensor("dnp", [128, 512], F32))

            with (nc.Block() if phases >= 2 else _NullBlock()) as blk:
                @blk.gpsimd
                def _(g):
                    for p in range(32):
                        k, j = p // 8, p % 8
                        P = SLOT_P[k]
                        if p >= 2:
                            g.wait_ge(S["pv"].h, p - 1)
                        g.dma_start(out=ktw[p % 2][:, 0:128 * P],
                                    in_=kt_d.ap()[j, :, 0:128 * P]).then_inc(S["ktl"].h, 16); S["ktl"].n += 16
                        g.dma_start(out=vw[p % 2][:, 0:128 * P].rearrange("p (a f) -> p a f", a=P),
                                    in_=v_d.ap()[0:P, :, 128 * j:128 * (j + 1)].rearrange("a p f -> p a f")
                                    ).then_inc(S["vl"].h, 16); S["vl"].n += 16
                        for _ in range(2):
                            if wqst["mload"] < 64:
                                emit_wload(g)

                @blk.sync
                def _(sy):
                    for k in range(4):
                        if k >= 1:
                            if k >= 2:
                                sy.wait_ge(S["tsc"].h, 2 * TSC_CUM[k - 1])
                            sy.dma_start(out=mskb[k % 2][:, :].rearrange("p (a f) -> p a f", a=8),
                                         in_=masks.ap()[k, :, :, :].rearrange("a p f -> p a f")).then_inc(S["mkl"].h, 16); S["mkl"].n += 16
                        for _ in range(16):
                            if wqst["nwq"] < 64:
                                emit_wq_step(sy)
                        for i in range(4):
                            n = 4 * k + i
                            if n >= 1:
                                sy.wait_ge(S["res"].h, 3 * n)
                            sy.dma_start(out=xres[:, :],
                                         in_=x_own.ap()[512 * k + 128 * i:512 * k + 128 * (i + 1), :]).then_inc(S["xrl"].h, 16); S["xrl"].n += 16
                            if n >= 1:
                                ti = n - 1
                                sy.wait_ge(S["xt2"].h, ti + 1)
                                sy.dma_start_transpose(x2T[:, :, 128 * ti:128 * (ti + 1)],
                                                       x2stg[ti % 2][:, :]).then_inc(S["xqt"].h, 16); S["xqt"].n += 16
                                sy.dma_start(out=x2_d.ap()[128 * ti:128 * (ti + 1), :],
                                             in_=x2stg[ti % 2][:, :]).then_inc(S["xqt"].h, 16); S["xqt"].n += 16

                @blk.tensor
                def _(t):
                    t.wait_ge(S["ldc"].h, S["ldc"].n)
                    msk_base = [0]
                    for k in range(4):
                        P = SLOT_P[k]
                        grps = GRPS[k]
                        G = NG[k]
                        t.wait_ge(S["mkl"].h, 16 * (k + 1))
                        for j in range(8):
                            p = 8 * k + j
                            t.wait_ge(S["ktl"].h, 16 * (p + 1))
                            t.wait_ge(S["vl"].h, 16 * (p + 1))
                            kw, vv = ktw[p % 2], vw[p % 2]
                            qc0 = j * TQ + 512 * k
                            base = TSC_CUM[k] + j * G
                            gate = []
                            if p >= 1:
                                gate.append((S["div"].h, p))
                            if j == 0 and k >= 1:
                                gate.append((S["res"].h, 12 * k - 1))
                            pv_gate = [gate if gate else None]

                            def pv_den(gi, final):
                                if pv_gate[0] is not None:
                                    for sem, val in pv_gate[0]:
                                        t.wait_ge(sem, val)
                                    pv_gate[0] = None
                                grp = grps[gi]
                                eA, eB = expb[gi % 2]
                                for li, sc in enumerate(grp):
                                    st0 = (sc == 0)
                                    fin = final and (li == len(grp) - 1)
                                    t.matmul(pvp[0:64, :], vv[:, 128 * sc:128 * sc + 64],
                                             eA[:, 512 * li:512 * (li + 1)],
                                             start=st0, stop=fin, tile_position=(0, 0),
                                             skip_group_check=True)
                                    t.matmul(pvp[64:128, :], vv[:, 128 * sc + 64:128 * sc + 128],
                                             eB[:, 512 * li:512 * (li + 1)],
                                             start=st0, stop=fin, tile_position=(0, 64),
                                             skip_group_check=True)
                                    t.matmul(dnp[0:64, :], onesb[:, :],
                                             eA[:, 512 * li:512 * (li + 1)],
                                             start=st0, stop=fin, tile_position=(0, 0),
                                             skip_group_check=True)
                                    mm = t.matmul(dnp[64:128, :], onesb[:, :],
                                                  eB[:, 512 * li:512 * (li + 1)],
                                                  start=st0, stop=fin, tile_position=(0, 64),
                                                  skip_group_check=True)
                                    if fin:
                                        mm.then_inc(S["pv"].h, 1); S["pv"].n += 1

                            def grp_masked(gi):
                                return [sc for sc in grps[gi]
                                        if sc >= P - 8 and MW[k][sc - (P - 8)] > 0]

                            def nmsk_through(gi):
                                n = msk_base[0]
                                for g2 in range(gi + 1):
                                    if grp_masked(g2):
                                        n += 2
                                return n

                            for gi in range(G):
                                gg = base + gi
                                grp = grps[gi]
                                for hh in range(2):
                                    if gg + hh >= 1:
                                        t.wait_ge(S["exp"].h, 2 * gg - 1 + hh)
                                    pss = psA if hh == 0 else psB
                                    for li, sc in enumerate(grp):
                                        mm = t.matmul(pss[:, 512 * li:512 * (li + 1)],
                                                      kw[64 * hh:64 * hh + 64, 128 * sc:128 * (sc + 1)],
                                                      qT[64 * hh:64 * hh + 64, qc0:qc0 + 512],
                                                      start=True, stop=True,
                                                      tile_position=(64 * hh, 0),
                                                      skip_group_check=True)
                                        if li == len(grp) - 1:
                                            mm.then_inc(S["tsc"].h, 1); S["tsc"].n += 1
                                if gi >= 1:
                                    if grp_masked(gi - 1):
                                        t.wait_ge(S["msk"].h, nmsk_through(gi - 1))
                                    pv_den(gi - 1, final=False)
                            t.wait_ge(S["exp"].h, 2 * (base + G))
                            if grp_masked(G - 1):
                                t.wait_ge(S["msk"].h, nmsk_through(G - 1))
                            pv_den(G - 1, final=True)
                            msk_base[0] = nmsk_through(G - 1)
                        for i in range(4):
                            for oh in range(2):
                                ch = 2 * i + oh
                                if ch == 0:
                                    t.wait_ge(S["div"].h, 8 * (k + 1))
                                if i >= 1:
                                    t.wait_ge(S["res"].h, 12 * k + 3 * (i - 1) + oh + 1)
                                pso = pvp if ch % 2 == 0 else dnp
                                for c in range(8):
                                    mm = t.matmul(pso[:, 0:512],
                                                  outT[:, 512 * c + 128 * i:512 * c + 128 * (i + 1)],
                                                  woutT_s[:, c * D + 512 * oh: c * D + 512 * (oh + 1)],
                                                  start=(c == 0), stop=(c == 7), skip_group_check=True)
                                mm.then_inc(S["op"].h, 1); S["op"].n += 1

                @blk.scalar
                def _(s):
                    for k in range(4):
                        G = NG[k]
                        grps = GRPS[k]
                        for j in range(8):
                            p = 8 * k + j
                            base = TSC_CUM[k] + j * G
                            for gi in range(G):
                                gg = base + gi
                                L = len(grps[gi])
                                s.wait_ge(S["tsc"].h, 2 * gg + 1)
                                s.activation(expb[gi % 2][0][:, 0:512 * L], psA[:, 0:512 * L],
                                             AF.Exp, scale=0.125).then_inc(S["exp"].h, 1); S["exp"].n += 1
                                s.wait_ge(S["tsc"].h, 2 * gg + 2)
                                s.activation(expb[gi % 2][1][:, 0:512 * L], psB[:, 0:512 * L],
                                             AF.Exp, scale=0.125).then_inc(S["exp"].h, 1); S["exp"].n += 1
                            s.wait_ge(S["pv"].h, p + 1)
                            s.activation(rcpa[:, :], dnp[:, :], AF.Ln)
                            s.activation(rcpb[:, :], rcpa[:, :], AF.Exp, scale=-1.0).then_inc(S["rcp"].h, 1); S["rcp"].n += 1
                        for i in range(4):
                            s.wait_ge(S["res"].h, 12 * k + 3 * i + 3)
                            s.activation(mv[i % 2][:, 2:3], mv[i % 2][:, 1:2], AF.Ln, bias=epsln[:, :])
                            s.drain()
                            s.activation(mv[i % 2][:, 3:4], mv[i % 2][:, 2:3], AF.Exp, scale=-0.5).then_inc(S["lnr"].h, 1); S["lnr"].n += 1

                @blk.vector
                def _(v):
                    for k in range(4):
                        P = SLOT_P[k]
                        grps = GRPS[k]
                        G = NG[k]
                        for j in range(8):
                            p = 8 * k + j
                            base = TSC_CUM[k] + j * G
                            for gi in range(G):
                                gg = base + gi
                                grp = grps[gi]
                                msk = [(li, sc) for li, sc in enumerate(grp)
                                       if sc >= P - 8 and MW[k][sc - (P - 8)] > 0]
                                if not msk:
                                    continue
                                for hh in range(2):
                                    v.wait_ge(S["exp"].h, 2 * gg + 1 + hh)
                                    eT = expb[gi % 2][hh]
                                    for li, sc in msk:
                                        r = sc - (P - 8)
                                        w = MW[k][r]
                                        mi = v.tensor_tensor(eT[:, 512 * li:512 * li + w],
                                                             eT[:, 512 * li:512 * li + w],
                                                             mskb[k % 2][:, 512 * r:512 * r + w],
                                                             ALU.mult)
                                    mi.then_inc(S["msk"].h, 1); S["msk"].n += 1
                            if j == 0:
                                v.wait_ge(S["op"].h, 8 * k)
                            v.wait_ge(S["rcp"].h, p + 1)
                            v.tensor_tensor(outT[:, 512 * j:512 * (j + 1)], pvp[:, :], rcpb[:, :],
                                            ALU.mult).then_inc(S["div"].h, 1); S["div"].n += 1
                            while wqst["mq"] < min(64, 2 * (p + 1)):
                                if wqst["mq"] < 32:
                                    emit_w1_quant(v)
                                else:
                                    emit_w2_group(v)
                        def post_ln(i):
                            ti = 4 * k + i
                            v.wait_ge(S["lnr"].h, ti + 1)
                            v.tensor_tensor(mv[i % 2][:, 2:3], mv[i % 2][:, 0:1], mv[i % 2][:, 3:4], ALU.mult)
                            v.drain()
                            if ti >= 2:
                                v.wait_ge(S["xqt"].h, 32 * (ti - 1))
                            v.tensor_scalar(x2stg[ti % 2][:, :], xr[i % 2][:, :], mv[i % 2][:, 3:4], mv[i % 2][:, 2:3],
                                            ALU.mult, ALU.subtract)
                            v.reduce_max(x2max[:, ti:ti + 1], x2stg[ti % 2][:, :],
                                         axis=AX.X).then_inc(S["xt2"].h, 1); S["xt2"].n += 1

                        for i in range(4):
                            for oh in range(2):
                                v.wait_ge(S["op"].h, 8 * k + 2 * i + oh + 1)
                                v.wait_ge(S["xrl"].h, 16 * (4 * k + i + 1))
                                pso = pvp if (2 * i + oh) % 2 == 0 else dnp
                                v.tensor_tensor(xr[i % 2][:, 512 * oh:512 * (oh + 1)], pso[:, 0:512],
                                                xres[:, 512 * oh:512 * (oh + 1)], ALU.add).then_inc(S["res"].h, 1); S["res"].n += 1
                            v.bn_stats(bns[:, 0:6], xr[i % 2][:, 0:512])
                            v.bn_stats(bns[:, 6:12], xr[i % 2][:, 512:1024])
                            v.drain()
                            v.bn_aggr(mv[i % 2][:, 0:2], bns[:, :]).then_inc(S["res"].h, 1); S["res"].n += 1
                            if i >= 1:
                                post_ln(i - 1)
                        post_ln(3)

        pw2.close()

        with ExitStack() as p3:
            E3 = p3.enter_context
            xqT = x2T
            q32 = E3(nc.sbuf_tensor("q32", [128, D], F32))
            qi32 = E3(nc.sbuf_tensor("qi32", [128, D], I32))
            sS = E3(nc.sbuf_tensor("sS", [128, 32], F32))
            sSi = E3(nc.sbuf_tensor("sSi", [128, 32], F32))
            q32w = E3(nc.sbuf_tensor("q32w", [128, 8, 512], F32))
            qi32w = E3(nc.sbuf_tensor("qi32w", [128, 8, 512], I32))
            w1blk = [E3(nc.sbuf_tensor(f"w1blk{i}", [128, 1024], BF16)) for i in range(2)]
            hsb = [E3(nc.sbuf_tensor(f"hsb{i}", [128, TQ], BF16)) for i in range(2)]
            f1ps = [E3(nc.psum_tensor(f"f1ps{i}", [128, 512], F32)) for i in range(8)]

            with (nc.Block() if phases >= 3 else _NullBlock()) as blk:
                if phases >= 4:
                    @blk.gpsimd
                    def _(g):
                        for jf in range(2):
                            g.dma_start(out=w1blk[jf][:, :].rearrange("p (a f) -> p a f", a=8),
                                        in_=w1qt_d.ap()[jf, :, :, :]).then_inc(S["w1l"].h, 16); S["w1l"].n += 16

                @blk.vector
                def _(v):
                    v.drain()
                    v.reduce_max(red1[:, 0:1], x2max[:, :], axis=AX.X)
                    v.drain()
                    v.nop().then_inc(S["v3"].h, 1); S["v3"].n += 1

                @blk.gpsimd
                def _(g):
                    g.wait_ge(S["v3"].h, 1)
                    g.dma_start(out=bass.AP(b2row_d, 0, [[1, 128]]), in_=red1[:, 0:1]).then_inc(S["gp"].h, 16); S["gp"].n += 16
                    g.wait_ge(S["gp"].h, S["gp"].n)
                    g.dma_start(out=q32[0:1, 0:128], in_=bass.AP(b2row_d, 0, [[128, 1], [1, 128]])).then_inc(S["gp"].h, 16); S["gp"].n += 16
                    g.wait_ge(S["gp"].h, S["gp"].n)
                    g.nop().then_inc(S["gp"].h, 1); S["gp"].n += 1

                @blk.vector
                def _(v):
                    v.wait_ge(S["gp"].h, S["gp"].n)
                    v.reduce_max(tiny[0:1, 0:1], q32[0:1, 0:128], axis=AX.X)
                    v.drain()
                    v.nop().then_inc(S["v3"].h, 1); S["v3"].n += 1

                @blk.gpsimd
                def _(g):
                    g.wait_ge(S["v3"].h, 2)
                    g.dma_start(out=cc_in.ap(), in_=tiny[0:1, :]).then_inc(S["gp"].h, 16); S["gp"].n += 16
                    g.wait_ge(S["gp"].h, S["gp"].n)
                    g.collective_compute("AllReduce", ALU.max, replica_groups=[list(range(NC))],
                                         ins=[cc_in.ap().opt()], outs=[cc_out.ap().opt()]).then_inc(S["cc"].h, 1); S["cc"].n += 1
                    g.wait_ge(S["cc"].h, S["cc"].n)
                    g.dma_start(out=tiny2[0:1, :], in_=cc_out.ap()).then_inc(S["gp"].h, 16); S["gp"].n += 16

                @blk.vector
                def _(v):
                    v.wait_ge(S["gp"].h, S["gp"].n)
                    v.tensor_scalar(tiny[0:1, 0:1], tiny2[0:1, 0:1], EPS_Q, 1.0 / 255.0, ALU.max, ALU.mult)
                    v.drain()
                    v.reciprocal(tiny[0:1, 1:2], tiny[0:1, 0:1])
                    v.drain()
                    v.nop().then_inc(S["v3"].h, 1); S["v3"].n += 1

                @blk.gpsimd
                def _(g):
                    g.wait_ge(S["v3"].h, 3)
                    g.dma_start(out=bc_d.ap(), in_=tiny[0:1, :]).then_inc(S["gp"].h, 16); S["gp"].n += 16
                    g.wait_ge(S["gp"].h, S["gp"].n)
                    g.dma_start(out=scal8[:, :], in_=bass.AP(bc_d, 0, [[0, 128], [1, 8]])).then_inc(S["gp"].h, 16); S["gp"].n += 16

                @blk.vector
                def _(v):
                    v.wait_ge(S["gp"].h, S["gp"].n)
                    v.tensor_scalar(sS[:, :], sw1[:, :], scal8[:, 0:1], None, ALU.mult)
                    v.drain()
                    v.reciprocal(sSi[:, :], sS[:, :])
                    v.drain()
                    v.tensor_tensor(q32[:, 0:32], b1s[:, :], sSi[:, :], ALU.mult)
                    v.drain()
                    v.tensor_copy(qi32[:, 0:32], q32[:, 0:32])
                    v.drain()
                    v.tensor_tensor(b1qs[:, :], qi32[:, 0:32], sS[:, :], ALU.mult).then_inc(S["v3"].h, 1); S["v3"].n += 1
                    for w in range(4):
                        if w == 3:
                            v.wait_ge(S["xqt"].h, 32 * 16)
                        src_ap = x2T[:, :, 512 * w:512 * (w + 1)]
                        v.tensor_scalar(q32w[:, :, :], src_ap, scal8[:, 1:2], None, ALU.mult)
                        v.tensor_copy(qi32w[:, :, :], q32w[:, :, :])
                        v.tensor_scalar(q32w[:, :, :], qi32w[:, :, :], 0.0, 255.0, ALU.max, ALU.min)
                        v.tensor_scalar(src_ap, q32w[:, :, :], scal8[:, 0:1], None,
                                        ALU.mult).then_inc(S["xql"].h, 1); S["xql"].n += 1

                @blk.sync
                def _(sy):
                    ti = 15
                    sy.wait_ge(S["xt2"].h, ti + 1)
                    sy.dma_start_transpose(x2T[:, :, 128 * ti:128 * (ti + 1)],
                                           x2stg[ti % 2][:, :]).then_inc(S["xqt"].h, 16); S["xqt"].n += 16
                    sy.dma_start(out=x2_d.ap()[128 * ti:128 * (ti + 1), :],
                                 in_=x2stg[ti % 2][:, :]).then_inc(S["xqt"].h, 16); S["xqt"].n += 16

                if phases >= 4:
                    @blk.gpsimd
                    def _(g):
                        for jf in range(2, 32):
                            g.wait_ge(S["f1"].h, 4 * (jf - 1))
                            g.dma_start(out=w1blk[jf % 2][:, :].rearrange("p (a f) -> p a f", a=8),
                                        in_=w1qt_d.ap()[jf, :, :, :]).then_inc(S["w1l"].h, 16); S["w1l"].n += 16

                    @blk.tensor
                    def _(t):
                        for jf in range(32):
                            t.wait_ge(S["w1l"].h, 16 * (jf + 1))
                            if jf >= 2:
                                t.wait_ge(S["hev"].h, 4 * (jf - 1))
                            bs = 4 * (jf % 2)
                            for w in range(4):
                                if jf == 0:
                                    t.wait_ge(S["xql"].h, w + 1)
                                for c in range(8):
                                    mm = t.matmul(f1ps[bs + w][:, :],
                                                  w1blk[jf % 2][:, 128 * c:128 * (c + 1)],
                                                  xqT[:, c:c + 1, 512 * w:512 * (w + 1)],
                                                  start=(c == 0), stop=(c == 7))
                                mm.then_inc(S["f1"].h, 1); S["f1"].n += 1

                    @blk.vector
                    def _(v):
                        for jf in range(32):
                            if jf >= 2:
                                v.wait_ge(S["hsp"].h, 16 * (jf - 1))
                            bs = 4 * (jf % 2)
                            for w in range(4):
                                nch = 4 * jf + w
                                v.wait_ge(S["f1"].h, 4 * jf + w + 1)
                                v.tensor_scalar(hsb[jf % 2][:, 512 * w:512 * (w + 1)], f1ps[bs + w][:, :],
                                                b1qs[:, jf:jf + 1], NEG, ALU.add, ALU.max,
                                                accum_out=hstat[:, nch:nch + 1]).then_inc(S["hev"].h, 1); S["hev"].n += 1

                    @blk.sync
                    def _(sy):
                        for jf in range(32):
                            sy.wait_ge(S["hev"].h, 4 * (jf + 1))
                            sy.dma_start(out=h_d.ap()[jf, :, :], in_=hsb[jf % 2][:, :]).then_inc(S["hsp"].h, 16); S["hsp"].n += 16

            with (nc.Block() if phases >= 5 else _NullBlock()) as blk:
                @blk.vector
                def _(v):
                    v.reduce_max(red1[:, 0:1], hstat[:, :], axis=AX.X)
                    v.drain()
                    v.nop().then_inc(S["v5"].h, 1); S["v5"].n += 1

                @blk.gpsimd
                def _(g):
                    g.wait_ge(S["v5"].h, 1)
                    g.dma_start(out=bass.AP(b2row_d, 0, [[1, 128]]), in_=red1[:, 0:1]).then_inc(S["gp"].h, 16); S["gp"].n += 16
                    g.wait_ge(S["gp"].h, S["gp"].n)
                    g.dma_start(out=q32[0:1, 0:128], in_=bass.AP(b2row_d, 0, [[128, 1], [1, 128]])).then_inc(S["gp"].h, 16); S["gp"].n += 16
                    g.wait_ge(S["gp"].h, S["gp"].n)
                    g.nop().then_inc(S["gp"].h, 1); S["gp"].n += 1

                @blk.vector
                def _(v):
                    v.wait_ge(S["gp"].h, S["gp"].n)
                    v.reduce_max(tiny[0:1, 0:1], q32[0:1, 0:128], axis=AX.X)
                    v.drain()
                    v.nop().then_inc(S["v5"].h, 1); S["v5"].n += 1

                @blk.gpsimd
                def _(g):
                    g.wait_ge(S["v5"].h, 2)
                    g.dma_start(out=cc_in.ap(), in_=tiny[0:1, :]).then_inc(S["gp"].h, 16); S["gp"].n += 16
                    g.wait_ge(S["gp"].h, S["gp"].n)
                    g.collective_compute("AllReduce", ALU.max, replica_groups=[list(range(NC))],
                                         ins=[cc_in.ap().opt()], outs=[cc_out.ap().opt()]).then_inc(S["cc"].h, 1); S["cc"].n += 1
                    g.wait_ge(S["cc"].h, S["cc"].n)
                    g.dma_start(out=tiny2[0:1, :], in_=cc_out.ap()).then_inc(S["gp"].h, 16); S["gp"].n += 16

                @blk.vector
                def _(v):
                    v.wait_ge(S["gp"].h, S["gp"].n)
                    v.tensor_scalar(tiny[0:1, 2:3], tiny2[0:1, 0:1], EPS_Q, 1.0 / 255.0, ALU.max, ALU.mult)
                    v.drain()
                    v.reciprocal(tiny[0:1, 3:4], tiny[0:1, 2:3])
                    v.drain()
                    v.nop().then_inc(S["v5"].h, 1); S["v5"].n += 1

                @blk.gpsimd
                def _(g):
                    g.wait_ge(S["v5"].h, 3)
                    g.dma_start(out=bc_d.ap(), in_=tiny[0:1, :]).then_inc(S["gp"].h, 16); S["gp"].n += 16
                    g.wait_ge(S["gp"].h, S["gp"].n)
                    g.dma_start(out=scal8[:, :], in_=bass.AP(bc_d, 0, [[0, 128], [1, 8]])).then_inc(S["gp"].h, 16); S["gp"].n += 16

                @blk.vector
                def _(v):
                    v.wait_ge(S["gp"].h, S["gp"].n)
                    v.tensor_scalar(sS[:, 0:8], sw2[:, :], scal8[:, 2:3], None, ALU.mult)
                    v.drain()
                    v.reciprocal(sSi[:, 0:8], sS[:, 0:8])
                    v.drain()
                    v.tensor_tensor(q32[:, 0:8], b2s[:, :], sSi[:, 0:8], ALU.mult)
                    v.drain()
                    v.tensor_copy(qi32[:, 0:8], q32[:, 0:8])
                    v.drain()
                    v.tensor_tensor(b2qs[:, :], qi32[:, 0:8], sS[:, 0:8], ALU.mult).then_inc(S["v5"].h, 1); S["v5"].n += 1

                @blk.gpsimd
                def _(g):
                    g.wait_ge(S["v5"].h, 4)
                    g.dma_start(out=bass.AP(b2row_d, 0, [[1, 128], [128, 8]]),
                                in_=b2qs[:, :], allow_slow_non_contiguous=True).then_inc(S["gp"].h, 16); S["gp"].n += 16
                    g.wait_ge(S["gp"].h, S["gp"].n)
                    g.dma_start(out=b2qb[:, :], in_=bass.AP(b2row_d, 0, [[0, 128], [1, D]])).then_inc(S["gp"].h, 16); S["gp"].n += 16
                    g.wait_ge(S["gp"].h, S["gp"].n)
                    if debug:
                        for nm, src in [("b2qb", b2qb[:, :]), ("b1q", b1qs[:, :]),
                                        ("hstat", hstat[:, :]), ("sw1", sw1[:, :]), ("sw2", sw2[:, :]),
                                        ("s12", bc_d.ap()),
                                        ("xqT", xqT[:, :, :].rearrange("p a f -> p (a f)"))]:
                            g.dma_start(out=dbg[nm].ap(), in_=src).then_inc(S["gp"].h, 16); S["gp"].n += 16
                        g.wait_ge(S["gp"].h, S["gp"].n)

        with ExitStack() as p6:
            E6 = p6.enter_context
            hload = [E6(nc.sbuf_tensor(f"hload{i}", [128, 1024], BF16)) for i in range(4)]
            x2ld = [E6(nc.sbuf_tensor(f"x2ld{i}", [128, D], BF16)) for i in range(2)]
            hq = E6(nc.sbuf_tensor("hq", [128, 32 * 1024], BF16))
            w2blk = [E6(nc.sbuf_tensor(f"w2blk{i}", [128, 512], BF16)) for i in range(4)]
            y2sb = E6(nc.sbuf_tensor("y2sb", [128, 8 * D], F32))
            q32b = E6(nc.sbuf_tensor("q32b", [128, 1024], F32))
            qi32b = E6(nc.sbuf_tensor("qi32b", [128, 1024], I32))
            xr6 = E6(nc.sbuf_tensor("xr6", [128, D], F32))
            outt = [E6(nc.sbuf_tensor(f"outt{i}", [128, D], F32)) for i in range(2)]
            bns6 = E6(nc.sbuf_tensor("bns6", [128, 12], F32))
            mv6 = E6(nc.sbuf_tensor("mv6", [128, 4], F32))
            y2ps = [E6(nc.psum_tensor(f"y2ps{i}", [128, 512], F32)) for i in range(8)]

            EVB = [0, 8, 24, 32]

            with (nc.Block() if phases >= 6 else _NullBlock()) as blk:
                @blk.gpsimd
                def _(g):
                    for n in range(64):
                        if n >= 4:
                            g.wait_ge(S["hq"].h, n - 3)
                        tb, jf = n // 32, n % 32
                        g.dma_start(out=hload[n % 4][:, :],
                                    in_=h_d.ap()[jf, :, 1024 * tb:1024 * (tb + 1)]).then_inc(S["hl"].h, 16); S["hl"].n += 16
                    if debug:
                        for nm, src in [("qT", qT[:, :]),
                                        ("kt0", kt_d.ap()[0, :, :]), ("v0", v_d.ap()[0, :, :]),
                                        ("h0", h_d.ap()[0, :, :])]:
                            g.dma_start(out=dbg[nm].ap(), in_=src).then_inc(S["gp"].h, 16); S["gp"].n += 16
                        g.dma_start(out=dbg["x2"].ap().rearrange("p (a d) -> p a d", a=16),
                                    in_=x2_d.ap().rearrange("(a p) d -> p a d", p=128)).then_inc(S["gp"].h, 16); S["gp"].n += 16
                        g.wait_ge(S["gp"].h, S["gp"].n)

                @blk.sync
                def _(sy):
                    for tb in range(2):
                        for oh in range(2):
                            for jf in range(32):
                                n = 64 * tb + 32 * oh + jf
                                if n >= 4:
                                    sy.wait_ge(S["f2"].h, n - 3)
                                sy.dma_start(out=w2blk[n % 4][:, :],
                                             in_=w2qt_d.ap()[jf, :, 512 * oh:512 * (oh + 1)]).then_inc(S["w2l"].h, 16); S["w2l"].n += 16
                        for i in range(8):
                            ti = 8 * tb + i
                            sy.wait_ge(S["ot"].h, ti + 1)
                            sy.dma_start(out=out.ap()[128 * ti:128 * (ti + 1), :],
                                         in_=outt[ti % 2][:, :]).then_inc(S["o"].h, 16); S["o"].n += 16
                    sy.wait_ge(S["o"].h, S["o"].n)

                @blk.vector
                def _(v):
                    def hquant(n):
                        jf = n % 32
                        v.wait_ge(S["hl"].h, 16 * (n + 1))
                        if n >= 32:
                            v.wait_ge(S["f2"].h, 32 + jf + 1)
                        v.tensor_scalar(q32b[:, :], hload[n % 4][:, :], scal8[:, 3:4], None, ALU.mult)
                        v.tensor_copy(qi32b[:, :], q32b[:, :])
                        v.tensor_scalar(q32b[:, :], qi32b[:, :], 0.0, 255.0, ALU.max, ALU.min)
                        v.tensor_scalar(hq[:, 1024 * jf:1024 * (jf + 1)], q32b[:, :], scal8[:, 2:3], None,
                                        ALU.mult).then_inc(S["hq"].h, 1); S["hq"].n += 1

                    for tb in range(2):
                        if tb == 0:
                            for n in range(32):
                                hquant(n)
                        for oh in range(2):
                            if tb == 0 and oh == 1:
                                for n in range(32, 56):
                                    hquant(n)
                            v.wait_ge(S["f2"].h, 32 * (2 * tb + oh + 1))
                            for i in range(8):
                                v.tensor_tensor(y2sb[:, D * i + 512 * oh:D * i + 512 * (oh + 1)],
                                                y2ps[i][:, :], b2qb[:, 512 * oh:512 * (oh + 1)],
                                                ALU.add).then_inc(S["y2e"].h, 1); S["y2e"].n += 1
                            if tb == 0 and oh == 1:
                                for n in range(56, 64):
                                    hquant(n)
                        for i in range(8):
                            ti = 8 * tb + i
                            v.wait_ge(S["x2l"].h, 16 * (ti + 1))
                            v.tensor_tensor(xr6[:, :], y2sb[:, D * i:D * (i + 1)],
                                            x2ld[ti % 2][:, :], ALU.add)
                            v.bn_stats(bns6[:, 0:6], xr6[:, 0:512])
                            v.bn_stats(bns6[:, 6:12], xr6[:, 512:1024])
                            v.drain()
                            v.bn_aggr(mv6[:, 0:2], bns6[:, :]).then_inc(S["y2e"].h, 1); S["y2e"].n += 1
                            v.wait_ge(S["ln2"].h, ti + 1)
                            v.tensor_tensor(mv6[:, 2:3], mv6[:, 0:1], mv6[:, 3:4], ALU.mult)
                            v.drain()
                            if ti >= 2:
                                v.wait_ge(S["o"].h, 16 * (ti - 1))
                            v.tensor_scalar(outt[ti % 2][:, :], xr6[:, :], mv6[:, 3:4], mv6[:, 2:3],
                                            ALU.mult, ALU.subtract).then_inc(S["ot"].h, 1); S["ot"].n += 1

                @blk.tensor
                def _(t):
                    for tb in range(2):
                        for oh in range(2):
                            sec = 2 * tb + oh
                            for jf in range(32):
                                n = 64 * tb + 32 * oh + jf
                                t.wait_ge(S["hq"].h, 32 * tb + jf + 1)
                                t.wait_ge(S["w2l"].h, 16 * (n + 1))
                                for i in range(8):
                                    if jf == 0 and sec >= 1:
                                        t.wait_ge(S["y2e"].h, EVB[sec - 1] + i + 1)
                                    mm = t.matmul(y2ps[i][:, :],
                                                  hq[:, 1024 * jf + 128 * i:1024 * jf + 128 * (i + 1)],
                                                  w2blk[n % 4][:, :],
                                                  start=(jf == 0), stop=(jf == 31), skip_group_check=True)
                                mm.then_inc(S["f2"].h, 1); S["f2"].n += 1

                @blk.scalar
                def _(s):
                    for ti in range(2):
                        s.dma_start(out=x2ld[ti % 2][:, :],
                                    in_=x2_d.ap()[128 * ti:128 * (ti + 1), :]).then_inc(S["x2l"].h, 16); S["x2l"].n += 16
                    for tb in range(2):
                        for i in range(8):
                            ti = 8 * tb + i
                            s.wait_ge(S["y2e"].h, 24 * tb + 16 + i + 1)
                            s.activation(mv6[:, 2:3], mv6[:, 1:2], AF.Ln, bias=epsln[:, :])
                            s.drain()
                            s.activation(mv6[:, 3:4], mv6[:, 2:3], AF.Exp, scale=-0.5).then_inc(S["ln2"].h, 1); S["ln2"].n += 1
                            if ti + 2 < 16:
                                s.wait_ge(S["ot"].h, ti + 1)
                                s.dma_start(out=x2ld[ti % 2][:, :],
                                            in_=x2_d.ap()[128 * (ti + 2):128 * (ti + 3), :]).then_inc(S["x2l"].h, 16); S["x2l"].n += 16

        if debug and phases < 6:
            with nc.Block() as blk:
                @blk.gpsimd
                def _(g):
                    pairs = [("kt0", kt_d.ap()[0, :, :]), ("v0", v_d.ap()[0, :, :]),
                             ("qT", qT[:, :])]
                    pass
                    if phases >= 3:
                        pairs.append(("s12", bc_d.ap()))
                        pairs.append(("b1q", b1qs[:, :]))
                        pairs.append(("sw1", sw1[:, :]))
                    if phases >= 4:
                        pairs.append(("h0", h_d.ap()[0, :, :]))
                        pairs.append(("hstat", hstat[:, :]))
                    for nm, srcp in pairs:
                        g.dma_start(out=dbg[nm].ap(), in_=srcp).then_inc(S["gp"].h, 16); S["gp"].n += 16
                    g.wait_ge(S["gp"].h, S["gp"].n)

    return nc, dbg


def _host_masks(parity):
    m = np.zeros((4, 8, 128, 512), dtype=np.float32)
    for k in range(4):
        P = SLOT_P[k]
        T0 = 512 * QCH[parity][k]
        for r in range(8):
            S0 = 128 * (P - 8 + r)
            srow = S0 + np.arange(128)[:, None]
            tcol = T0 + np.arange(512)[None, :]
            m[k, r] = np.where(srow > tcol, 0.0, 1.0)
    return np.ascontiguousarray(m.astype(ml_dtypes.bfloat16))


_CACHE = {}


def kernel(L0_attn_x, L0_attn_W_qkv, L0_attn_W_out, L0_ffn_w1, L0_ffn_b1, L0_ffn_w2, L0_ffn_b2,
           debug=False):
    x = np.ascontiguousarray(np.asarray(L0_attn_x, dtype=np.float32))
    wqkvT_h = np.ascontiguousarray(np.asarray(L0_attn_W_qkv, dtype=np.float32).T)
    woutT_h = np.ascontiguousarray(np.asarray(L0_attn_W_out, dtype=np.float32).T)
    w1_h = np.ascontiguousarray(np.asarray(L0_ffn_w1, dtype=np.float32))
    b1_h = np.ascontiguousarray(np.asarray(L0_ffn_b1, dtype=np.float32).reshape(32, 128).T)
    w2_h = np.ascontiguousarray(np.asarray(L0_ffn_w2, dtype=np.float32))
    b2_h = np.ascontiguousarray(np.asarray(L0_ffn_b2, dtype=np.float32).reshape(8, 128).T)
    ident = np.ascontiguousarray(np.eye(128, dtype=np.float32).astype(ml_dtypes.bfloat16))

    key = ("prog", bool(debug), PHASES)
    if key not in _CACHE:
        _CACHE[key] = build_program(debug=bool(debug))
    nc, _ = _CACHE[key]

    in_maps = []
    for c in range(NC):
        b, parity = c // 2, c % 2
        xb = np.ascontiguousarray(x[b])
        rows = np.ascontiguousarray(
            np.concatenate([xb[512 * cq:512 * (cq + 1)] for cq in QCH[parity]], axis=0))
        in_maps.append({
            "x_kv": xb, "x_own": rows,
            "wqkvT": wqkvT_h, "woutT": woutT_h,
            "w1": w1_h, "b1": b1_h, "w2": w2_h, "b2": b2_h,
            "masks": _host_masks(parity), "ident_bf": ident,
        })
    res = run_bass_kernel_spmd(nc, in_maps, core_ids=list(range(NC)))
    outp = np.empty((B, T, D), dtype=np.float32)
    for c in range(NC):
        b, parity = c // 2, c % 2
        o = res.results[c]["out"]
        for k, cq in enumerate(QCH[parity]):
            outp[b, 512 * cq:512 * (cq + 1)] = o[512 * k:512 * (k + 1)]
    if debug:
        return outp, res
    return outp


# version 24 (final)
# speedup vs baseline: 1.0319x; 1.0319x over previous
import numpy as np
import ml_dtypes
from contextlib import ExitStack

import concourse.bass as bass
import concourse.mybir as mybir
from concourse.bass_utils import run_bass_kernel_spmd

F32 = mybir.dt.float32
BF16 = mybir.dt.bfloat16
I32 = mybir.dt.int32
AF = mybir.ActivationFunctionType
ALU = mybir.AluOpType
AX = mybir.AxisListType

B, T, D = 4, 4096, 1024
FF = 4096
NC = 8
TQ = 2048
SLOT_P = [8, 16, 24, 32]
QCH = {0: [0, 3, 4, 7], 1: [1, 2, 5, 6]}
EPS_Q = 1e-8
EPS_LN = 1e-5
NEG = -1e30


def groups_of(P):
    G = (P + 2) // 3
    sizes = [3] * (P - 2 * G) + [2] * (3 * G - P)
    out, s = [], 0
    for sz in sizes:
        out.append(list(range(s, s + sz)))
        s += sz
    assert s == P
    return out


GRPS = [groups_of(P) for P in SLOT_P]
NG = [len(g) for g in GRPS]
TSC_CUM = [0]
for _k in range(4):
    TSC_CUM.append(TSC_CUM[-1] + 8 * NG[_k])


def _mask_width(k, r):
    P = SLOT_P[k]
    c = P - 8 + r
    w = 0
    for parity in (0, 1):
        cq = QCH[parity][k]
        E = 4 * (cq + 1)
        if c >= E:
            w = 512
        else:
            d = 512 * cq - 128 * c
            w = max(w, max(0, min(512, 128 - d)))
    return w



MW = [[_mask_width(k, r) for r in range(8)] for k in range(4)]


class _NullEng:
    def __call__(self, f):
        return None


class _NullBlockInner:
    def __getattr__(self, name):
        return _NullEng()


class _NullBlock:
    def __enter__(self):
        return _NullBlockInner()

    def __exit__(self, *a):
        return False


class Sem:
    def __init__(self, ctx, nc, name):
        self.h = ctx.enter_context(nc.semaphore(name))
        self.n = 0


PHASES = 6


def build_program(debug=False, phases=None):
    phases = PHASES if phases is None else phases
    nc = bass.Bass("TRN2", target_bir_lowering=False)

    x_kv = nc.dram_tensor("x_kv", [T, D], F32, kind="ExternalInput")
    x_own = nc.dram_tensor("x_own", [TQ, D], F32, kind="ExternalInput")
    wqkvT = nc.dram_tensor("wqkvT", [D, 3 * D], F32, kind="ExternalInput")
    woutT = nc.dram_tensor("woutT", [D, D], F32, kind="ExternalInput")
    w1 = nc.dram_tensor("w1", [FF, D], F32, kind="ExternalInput")
    b1 = nc.dram_tensor("b1", [128, 32], F32, kind="ExternalInput")
    w2 = nc.dram_tensor("w2", [D, FF], F32, kind="ExternalInput")
    b2 = nc.dram_tensor("b2", [128, 8], F32, kind="ExternalInput")
    masks = nc.dram_tensor("masks", [4, 8, 128, 512], BF16, kind="ExternalInput")
    ident_bf = nc.dram_tensor("ident_bf", [128, 128], BF16, kind="ExternalInput")
    out = nc.dram_tensor("out", [TQ, D], F32, kind="ExternalOutput")

    kt_d = nc.dram_tensor("kt_d", [8, 128, T], BF16)
    v_d = nc.dram_tensor("v_d", [32, 128, D], BF16)
    w1qt_d = nc.dram_tensor("w1qt_d", [32, 128, 8, 128], BF16)
    w2qt_d = nc.dram_tensor("w2qt_d", [32, 128, D], BF16)
    h_d = nc.dram_tensor("h_d", [32, 128, TQ], BF16)
    x2_d = nc.dram_tensor("x2_d", [TQ, D], BF16)
    cc_in = nc.dram_tensor("cc_in", [1, 8], F32)
    cc_out = nc.dram_tensor("cc_out", [1, 8], F32)
    bc_d = nc.dram_tensor("bc_d", [1, 8], F32)
    b2row_d = nc.dram_tensor("b2row_d", [1, D], F32)

    dbg = {}
    if debug:
        for nm, shp in [("qT", [128, 8 * TQ]), ("x2", [128, 16 * D]),
                        ("xqT", [128, 8 * TQ]), ("s12", [1, 8]),
                        ("kt0", [128, T]), ("v0", [128, D]), ("h0", [128, TQ]),
                        ("hstat", [128, 128]), ("b1q", [128, 32]), ("b2qb", [128, D]),
                        ("sw1", [128, 32]), ("sw2", [128, 8])]:
            dbg[nm] = nc.dram_tensor("dbg_" + nm, shp, F32, kind="ExternalOutput")

    ctx = ExitStack()
    with ctx:
        E = ctx.enter_context
        qT = E(nc.sbuf_tensor("qT", [128, 8 * TQ], BF16))
        x2T = E(nc.sbuf_tensor("x2T", [128, 8, TQ], BF16))
        sw1 = E(nc.sbuf_tensor("sw1", [128, 32], F32))
        sw2 = E(nc.sbuf_tensor("sw2", [128, 8], F32))
        b1s = E(nc.sbuf_tensor("b1s", [128, 32], F32))
        b2s = E(nc.sbuf_tensor("b2s", [128, 8], F32))
        b1qs = E(nc.sbuf_tensor("b1qs", [128, 32], F32))
        b2qs = E(nc.sbuf_tensor("b2qs", [128, 8], F32))
        x2max = E(nc.sbuf_tensor("x2max", [128, 16], F32))
        hstat = E(nc.sbuf_tensor("hstat", [128, 128], F32))
        onesb = E(nc.sbuf_tensor("onesb", [128, 64], BF16))
        identb = E(nc.sbuf_tensor("identb", [128, 128], BF16))
        scal8 = E(nc.sbuf_tensor("scal8", [128, 8], F32))
        tiny = E(nc.sbuf_tensor("tiny", [1, 8], F32))
        tiny2 = E(nc.sbuf_tensor("tiny2", [1, 8], F32))
        b2qb = E(nc.sbuf_tensor("b2qb", [128, D], F32))
        red1 = E(nc.sbuf_tensor("red1", [128, 4], F32))
        epsln = E(nc.sbuf_tensor("epsln", [128, 1], F32))
        pw2 = ExitStack()
        woutT_s = pw2.enter_context(nc.sbuf_tensor("woutT_s", [128, 8 * D], BF16))
        mskb = [pw2.enter_context(nc.sbuf_tensor(f"mskb{i}", [128, 8 * 512], BF16)) for i in range(2)]

        S = {}
        for nm in ["ldx", "trx", "ldo", "tro", "ldc", "mm", "ev", "sp",
                   "ktl", "vl", "mkl", "xrl", "tsc", "exp", "pv", "rcp", "div",
                   "op", "res", "lnr", "xt2",
                   "wl", "qv", "wtr", "wsp", "msk",
                   "gp", "cc", "v3", "xql", "xqt",
                   "w1l", "f1", "hev", "hsp", "v5",
                   "hl", "hq", "w2l", "f2", "y2e", "ln2", "ot", "o", "x2l"]:
            S[nm] = Sem(ctx, nc, "s_" + nm)

        def _tail_dump():
            if not debug:
                return
            with nc.Block() as blk:
                @blk.gpsimd
                def _(g):
                    pairs = [("kt0", kt_d.ap()[0, :, :]), ("v0", v_d.ap()[0, :, :]),
                             ("qT", qT[:, :])]
                    pass
                    if phases >= 3:
                        pairs.append(("s12", bc_d.ap()))
                        pairs.append(("b1q", b1qs[:, :]))
                    if phases >= 4:
                        pairs.append(("h0", h_d.ap()[0, :, :]))
                        pairs.append(("hstat", hstat[:, :]))
                    for nm, srcp in pairs:
                        g.dma_start(out=dbg[nm].ap(), in_=srcp).then_inc(S["gp"].h, 16); S["gp"].n += 16
                    g.wait_ge(S["gp"].h, S["gp"].n)

        wqst = {"mload": 0, "nwq": 0, "mq": 0}

        def wload_src(m):
            if m < 32:
                return w1.ap()[128 * m:128 * (m + 1), :]
            q = m - 32
            jj, qq = q // 4, q % 4
            return w2.ap()[128 * jj:128 * (jj + 1), 1024 * qq:1024 * (qq + 1)]

        def emit_wload(eng):
            m = wqst["mload"]
            if m >= 4:
                eng.wait_ge(S["qv"].h, m - 3)
            eng.dma_start(out=wstg[m % 4][:, :], in_=wload_src(m)).then_inc(S["wl"].h, 16); S["wl"].n += 16
            wqst["mload"] = m + 1

        def emit_wq_step(sy):
            nwq = wqst["nwq"]
            sy.wait_ge(S["qv"].h, nwq + 1)
            if nwq >= 2:
                sy.wait_ge(S["wsp"].h, 16 * (nwq - 1))
            sy.dma_start_transpose(wtstg[nwq % 2][:, :, :],
                                   wqstg[nwq % 4][:, :]).then_inc(S["wtr"].h, 16); S["wtr"].n += 16
            sy.wait_ge(S["wtr"].h, S["wtr"].n)
            if nwq < 32:
                dst = w1qt_d.ap()[nwq, :, :, :]
            else:
                q = nwq - 32
                jj, qq = q // 4, q % 4
                dst = w2qt_d.ap()[8 * qq:8 * qq + 8, :, 128 * jj:128 * (jj + 1)].rearrange("a p f -> p a f")
            sy.dma_start(out=dst, in_=wtstg[nwq % 2][:, :, :]).then_inc(S["wsp"].h, 16); S["wsp"].n += 16
            wqst["nwq"] = nwq + 1

        def emit_w1_quant(v):
            j = wqst["mq"]
            v.wait_ge(S["wl"].h, 16 * (j + 1))
            if j >= 4:
                v.wait_ge(S["wtr"].h, 16 * (j - 3))
            src_t = wstg[j % 4]
            v.drain()
            v.reduce_max(red1[:, 0:1], src_t[:, :], axis=AX.X, apply_absolute_value=True)
            v.drain()
            v.tensor_scalar(sw1[:, j:j + 1], red1[:, 0:1], EPS_Q, 1.0 / 127.0, ALU.max, ALU.mult)
            v.drain()
            v.reciprocal(red1[:, 1:2], sw1[:, j:j + 1])
            v.drain()
            v.tensor_scalar(src_t[:, :], src_t[:, :], red1[:, 1:2], None, ALU.mult)
            v.tensor_copy(wint[:, :], src_t[:, :])
            v.tensor_scalar(wqstg[j % 4][:, :], wint[:, :], sw1[:, j:j + 1], None,
                            ALU.mult).then_inc(S["qv"].h, 1); S["qv"].n += 1
            wqst["mq"] = j + 1

        def emit_w2_group(v):
            q = wqst["mq"] - 32
            jj = q // 4
            v.wait_ge(S["wl"].h, 16 * (32 + 4 * jj + 4))
            if jj >= 1:
                v.wait_ge(S["wtr"].h, 16 * (32 + 4 * (jj - 1) + 4))
            v.drain()
            for qq in range(4):
                v.reduce_max(red1[:, qq:qq + 1], wstg[qq][:, :], axis=AX.X, apply_absolute_value=True)
            v.drain()
            v.tensor_tensor(red1[:, 0:1], red1[:, 0:1], red1[:, 1:2], ALU.max)
            v.tensor_tensor(red1[:, 2:3], red1[:, 2:3], red1[:, 3:4], ALU.max)
            v.drain()
            v.tensor_tensor(red1[:, 0:1], red1[:, 0:1], red1[:, 2:3], ALU.max)
            v.drain()
            v.tensor_scalar(sw2[:, jj:jj + 1], red1[:, 0:1], EPS_Q, 1.0 / 127.0, ALU.max, ALU.mult)
            v.drain()
            v.reciprocal(red1[:, 1:2], sw2[:, jj:jj + 1])
            v.drain()
            for qq in range(4):
                v.tensor_scalar(wstg[qq][:, :], wstg[qq][:, :], red1[:, 1:2], None, ALU.mult)
                v.tensor_copy(wint[:, :], wstg[qq][:, :])
                v.tensor_scalar(wqstg[(32 + 4 * jj + qq) % 4][:, :], wint[:, :], sw2[:, jj:jj + 1], None,
                                ALU.mult).then_inc(S["qv"].h, 1); S["qv"].n += 1
            wqst["mq"] += 4

        with ExitStack() as p1:
            E1 = p1.enter_context
            wqkvT_s = E1(nc.sbuf_tensor("wqkvT_s", [128, 8 * 3 * D], BF16))
            xtw = [E1(nc.sbuf_tensor(f"xtw{i}", [128, 8, 512], BF16)) for i in range(2)]
            xow = [E1(nc.sbuf_tensor(f"xow{i}", [128, 8, 512], BF16)) for i in range(3)]
            xstg = [E1(nc.sbuf_tensor(f"xstg{i}", [128, D], BF16)) for i in range(2)]
            xostg = [E1(nc.sbuf_tensor(f"xostg{i}", [128, D], BF16)) for i in range(2)]
            kvstg = [E1(nc.sbuf_tensor(f"kvstg{i}", [128, 512], BF16)) for i in range(8)]
            ps = [E1(nc.psum_tensor(f"p1ps{i}", [128, 512], F32)) for i in range(4)]

            with (nc.Block() if phases >= 1 else _NullBlock()) as blk:
                @blk.gpsimd
                def _(g):
                    g.dma_start(out=identb[:, :], in_=ident_bf.ap()).then_inc(S["ldc"].h, 16); S["ldc"].n += 16
                    g.dma_start(out=wqkvT_s[:, :].rearrange("p (a f) -> p a f", a=8),
                                in_=wqkvT.ap().rearrange("(a p) f -> p a f", p=128)).then_inc(S["ldc"].h, 16); S["ldc"].n += 16

                    def spills_for_window(w):
                        base = 16 * w
                        for u in range(16):
                            nsp = base + u + 1
                            g.wait_ge(S["ev"].h, nsp)
                            if u < 8:
                                g.dma_start(out=kt_d.ap()[u, :, 512 * w:512 * (w + 1)],
                                            in_=kvstg[(nsp - 1) % 8][:, :]).then_inc(S["sp"].h, 16)
                            else:
                                i, fh = (u - 8) // 2, (u - 8) % 2
                                g.dma_start(out=v_d.ap()[4 * w + i, :, 512 * fh:512 * (fh + 1)],
                                            in_=kvstg[(nsp - 1) % 8][:, :]).then_inc(S["sp"].h, 16)
                            S["sp"].n += 16

                    io = 0
                    for i in range(32):
                        if i >= 2:
                            g.wait_ge(S["trx"].h, 16 * (i - 1))
                        g.dma_start(out=xstg[i % 2][:, :],
                                    in_=x_kv.ap()[128 * i:128 * (i + 1), :]).then_inc(S["ldx"].h, 16); S["ldx"].n += 16
                        if i % 2 == 1 and io < 14:
                            if io >= 2:
                                g.wait_ge(S["tro"].h, 16 * (io - 1))
                            g.dma_start(out=xostg[io % 2][:, :],
                                        in_=x_own.ap()[128 * io:128 * (io + 1), :]).then_inc(S["ldo"].h, 16); S["ldo"].n += 16
                            io += 1
                        if i % 4 == 3 and i // 4 >= 1:
                            spills_for_window(i // 4 - 1)
                    spills_for_window(7)
                    g.dma_start(out=woutT_s[:, :].rearrange("p (a f) -> p a f", a=8),
                                in_=woutT.ap().rearrange("(a p) f -> p a f", p=128)).then_inc(S["ldc"].h, 16); S["ldc"].n += 16
                    g.dma_start(out=mskb[0][:, :].rearrange("p (a f) -> p a f", a=8),
                                in_=masks.ap()[0, :, :, :].rearrange("a p f -> p a f")).then_inc(S["mkl"].h, 16); S["mkl"].n += 16
                    for io in range(14, 16):
                        g.wait_ge(S["tro"].h, 16 * (io - 1))
                        g.dma_start(out=xostg[io % 2][:, :],
                                    in_=x_own.ap()[128 * io:128 * (io + 1), :]).then_inc(S["ldo"].h, 16); S["ldo"].n += 16

                @blk.sync
                def _(sy):
                    io = 0
                    for i in range(32):
                        sy.wait_ge(S["ldx"].h, 16 * (i + 1))
                        w, sl = i // 4, i % 4
                        if sl == 0 and w >= 2:
                            sy.wait_ge(S["mm"].h, 16 * (w - 1))
                        sy.dma_start_transpose(xtw[w % 2][:, :, 128 * sl:128 * (sl + 1)],
                                               xstg[i % 2][:, :]).then_inc(S["trx"].h, 16); S["trx"].n += 16
                        if i % 2 == 1 and io < 12:
                            sy.wait_ge(S["ldo"].h, 16 * (io + 1))
                            w2_, sl2 = io // 4, io % 4
                            sy.dma_start_transpose(xow[w2_ % 3][:, :, 128 * sl2:128 * (sl2 + 1)],
                                                   xostg[io % 2][:, :]).then_inc(S["tro"].h, 16); S["tro"].n += 16
                            io += 1
                    sy.wait_ge(S["mm"].h, 136)
                    for io in range(12, 16):
                        sy.wait_ge(S["ldo"].h, 16 * (io + 1))
                        sy.dma_start_transpose(xow[0][:, :, 128 * (io % 4):128 * (io % 4 + 1)],
                                               xostg[io % 2][:, :]).then_inc(S["tro"].h, 16); S["tro"].n += 16
                    sy.dma_start(out=b1s[:, :], in_=b1.ap()).then_inc(S["ldc"].h, 16); S["ldc"].n += 16
                    sy.dma_start(out=b2s[:, :], in_=b2.ap()).then_inc(S["ldc"].h, 16); S["ldc"].n += 16

                @blk.scalar
                def _(s):
                    nev = 0
                    for w in range(8):
                        for u in range(16):
                            nev += 1
                            s.wait_ge(S["mm"].h, nev)
                            if nev > 8:
                                s.wait_ge(S["sp"].h, 16 * (nev - 8))
                            s.copy(kvstg[(nev - 1) % 8][:, :], ps[(nev - 1) % 4][:, :]).then_inc(S["ev"].h, 1); S["ev"].n += 1
                    for w in range(4):
                        for f in range(8):
                            nev += 1
                            s.wait_ge(S["mm"].h, nev)
                            s.copy(qT[:, f * TQ + 512 * w: f * TQ + 512 * (w + 1)],
                                   ps[(nev - 1) % 4][:, :]).then_inc(S["ev"].h, 1); S["ev"].n += 1

                @blk.tensor
                def _(t):
                    t.wait_ge(S["ldc"].h, 32)
                    nmm = 0
                    for w in range(8):
                        t.wait_ge(S["trx"].h, 16 * 4 * (w + 1))
                        xw = xtw[w % 2]
                        for u in range(16):
                            nmm += 1
                            if nmm > 4:
                                t.wait_ge(S["ev"].h, nmm - 4)
                            for c in range(8):
                                last = (c == 7)
                                if u < 8:
                                    mm = t.matmul(ps[(nmm - 1) % 4][:, :],
                                                  wqkvT_s[:, c * 3 * D + D + 128 * u: c * 3 * D + D + 128 * (u + 1)],
                                                  xw[:, c:c + 1, :],
                                                  start=(c == 0), stop=last)
                                else:
                                    i, fh = (u - 8) // 2, (u - 8) % 2
                                    mm = t.matmul(ps[(nmm - 1) % 4][:, :],
                                                  xw[:, c:c + 1, 128 * i:128 * (i + 1)],
                                                  wqkvT_s[:, c * 3 * D + 2 * D + 512 * fh: c * 3 * D + 2 * D + 512 * (fh + 1)],
                                                  start=(c == 0), stop=last)
                                if last:
                                    mm.then_inc(S["mm"].h, 1); S["mm"].n += 1
                    for w in range(4):
                        t.wait_ge(S["tro"].h, 16 * 4 * (w + 1))
                        xw = xow[w % 3]
                        for f in range(8):
                            nmm += 1
                            if nmm > 4:
                                t.wait_ge(S["ev"].h, nmm - 4)
                            for c in range(8):
                                mm = t.matmul(ps[(nmm - 1) % 4][:, :],
                                              wqkvT_s[:, c * 3 * D + 128 * f: c * 3 * D + 128 * (f + 1)],
                                              xw[:, c:c + 1, :],
                                              start=(c == 0), stop=(c == 7))
                                if c == 7:
                                    mm.then_inc(S["mm"].h, 1); S["mm"].n += 1

                @blk.vector
                def _(v):
                    v.memset(onesb[:, :], 1.0)
                    v.memset(epsln[:, :], EPS_LN)
                    v.memset(x2max[:, :], NEG)
                    v.memset(tiny[:, :], NEG)
                    v.memset(tiny2[:, :], NEG)

        with ExitStack() as p2:
            E2 = p2.enter_context
            ktw = [E2(nc.sbuf_tensor(f"ktw{i}", [128, 32 * 128], BF16)) for i in range(2)]
            vw = [E2(nc.sbuf_tensor(f"vw{i}", [128, 32 * 128], BF16)) for i in range(2)]
            wstg = [E2(nc.sbuf_tensor(f"wstg{i}", [128, 1024], F32)) for i in range(4)]
            wqstg = [E2(nc.sbuf_tensor(f"wqstg{i}", [128, 1024], BF16)) for i in range(4)]
            wtstg = [E2(nc.sbuf_tensor(f"wtstg{i}", [128, 8, 128], BF16)) for i in range(2)]
            wint = E2(nc.sbuf_tensor("wint", [128, 1024], I32))
            expb = [[E2(nc.sbuf_tensor(f"expb{i}{hh}", [128, 1536], BF16)) for hh in range(2)] for i in range(2)]
            outT = E2(nc.sbuf_tensor("outT", [128, 8 * 512], BF16))
            rcpa = E2(nc.sbuf_tensor("rcpa", [128, 512], F32))
            rcpb = E2(nc.sbuf_tensor("rcpb", [128, 512], F32))
            xres = E2(nc.sbuf_tensor("xres", [128, D], F32))
            xr = E2(nc.sbuf_tensor("xr", [128, D], F32))
            x2stg = [E2(nc.sbuf_tensor(f"x2stg{i}", [128, D], BF16)) for i in range(2)]
            bns = E2(nc.sbuf_tensor("bns", [128, 12], F32))
            mv = E2(nc.sbuf_tensor("mv", [128, 4], F32))
            psA = E2(nc.psum_tensor("psA", [128, 1536], F32))
            psB = E2(nc.psum_tensor("psB", [128, 1536], F32))
            pvp = E2(nc.psum_tensor("pvp", [128, 512], F32))
            dnp = E2(nc.psum_tensor("dnp", [128, 512], F32))

            with (nc.Block() if phases >= 2 else _NullBlock()) as blk:
                @blk.gpsimd
                def _(g):
                    for p in range(32):
                        k, j = p // 8, p % 8
                        P = SLOT_P[k]
                        if p >= 2:
                            g.wait_ge(S["pv"].h, p - 1)
                        g.dma_start(out=ktw[p % 2][:, 0:128 * P],
                                    in_=kt_d.ap()[j, :, 0:128 * P]).then_inc(S["ktl"].h, 16); S["ktl"].n += 16
                        g.dma_start(out=vw[p % 2][:, 0:128 * P].rearrange("p (a f) -> p a f", a=P),
                                    in_=v_d.ap()[0:P, :, 128 * j:128 * (j + 1)].rearrange("a p f -> p a f")
                                    ).then_inc(S["vl"].h, 16); S["vl"].n += 16
                        if j == 1:
                            if k >= 1:
                                g.wait_ge(S["res"].h, 12 * k)
                            g.dma_start(out=xres[:, :],
                                        in_=x_own.ap()[512 * k:512 * k + 128, :]).then_inc(S["xrl"].h, 16); S["xrl"].n += 16
                        for _ in range(2):
                            if wqst["mload"] < 64:
                                emit_wload(g)

                @blk.sync
                def _(sy):
                    for k in range(4):
                        if k >= 1:
                            if k >= 2:
                                sy.wait_ge(S["tsc"].h, 2 * TSC_CUM[k - 1])
                            sy.dma_start(out=mskb[k % 2][:, :].rearrange("p (a f) -> p a f", a=8),
                                         in_=masks.ap()[k, :, :, :].rearrange("a p f -> p a f")).then_inc(S["mkl"].h, 16); S["mkl"].n += 16
                        for _ in range(16):
                            if wqst["nwq"] < 64:
                                emit_wq_step(sy)
                        for i in range(4):
                            n = 4 * k + i
                            if i >= 1:
                                sy.wait_ge(S["res"].h, 3 * n)
                                sy.dma_start(out=xres[:, :],
                                             in_=x_own.ap()[512 * k + 128 * i:512 * k + 128 * (i + 1), :]).then_inc(S["xrl"].h, 16); S["xrl"].n += 16
                            if n >= 1:
                                ti = n - 1
                                sy.wait_ge(S["xt2"].h, ti + 1)
                                sy.dma_start_transpose(x2T[:, :, 128 * ti:128 * (ti + 1)],
                                                       x2stg[ti % 2][:, :]).then_inc(S["xqt"].h, 16); S["xqt"].n += 16
                                sy.dma_start(out=x2_d.ap()[128 * ti:128 * (ti + 1), :],
                                             in_=x2stg[ti % 2][:, :]).then_inc(S["xqt"].h, 16); S["xqt"].n += 16

                @blk.tensor
                def _(t):
                    t.wait_ge(S["ldc"].h, S["ldc"].n)
                    msk_base = [0]
                    for k in range(4):
                        P = SLOT_P[k]
                        grps = GRPS[k]
                        G = NG[k]
                        t.wait_ge(S["mkl"].h, 16 * (k + 1))
                        for j in range(8):
                            p = 8 * k + j
                            t.wait_ge(S["ktl"].h, 16 * (p + 1))
                            t.wait_ge(S["vl"].h, 16 * (p + 1))
                            kw, vv = ktw[p % 2], vw[p % 2]
                            qc0 = j * TQ + 512 * k
                            base = TSC_CUM[k] + j * G
                            gate = []
                            if p >= 1:
                                gate.append((S["div"].h, p))
                            if j == 0 and k >= 1:
                                gate.append((S["res"].h, 12 * k - 1))
                            pv_gate = [gate if gate else None]

                            def pv_den(gi, final):
                                if pv_gate[0] is not None:
                                    for sem, val in pv_gate[0]:
                                        t.wait_ge(sem, val)
                                    pv_gate[0] = None
                                grp = grps[gi]
                                eA, eB = expb[gi % 2]
                                for li, sc in enumerate(grp):
                                    st0 = (sc == 0)
                                    fin = final and (li == len(grp) - 1)
                                    t.matmul(pvp[0:64, :], vv[:, 128 * sc:128 * sc + 64],
                                             eA[:, 512 * li:512 * (li + 1)],
                                             start=st0, stop=fin, tile_position=(0, 0),
                                             skip_group_check=True)
                                    t.matmul(pvp[64:128, :], vv[:, 128 * sc + 64:128 * sc + 128],
                                             eB[:, 512 * li:512 * (li + 1)],
                                             start=st0, stop=fin, tile_position=(0, 64),
                                             skip_group_check=True)
                                    t.matmul(dnp[0:64, :], onesb[:, :],
                                             eA[:, 512 * li:512 * (li + 1)],
                                             start=st0, stop=fin, tile_position=(0, 0),
                                             skip_group_check=True)
                                    mm = t.matmul(dnp[64:128, :], onesb[:, :],
                                                  eB[:, 512 * li:512 * (li + 1)],
                                                  start=st0, stop=fin, tile_position=(0, 64),
                                                  skip_group_check=True)
                                    if fin:
                                        mm.then_inc(S["pv"].h, 1); S["pv"].n += 1

                            def grp_masked(gi):
                                return [sc for sc in grps[gi]
                                        if sc >= P - 8 and MW[k][sc - (P - 8)] > 0]

                            def nmsk_through(gi):
                                n = msk_base[0]
                                for g2 in range(gi + 1):
                                    if grp_masked(g2):
                                        n += 2
                                return n

                            for gi in range(G):
                                gg = base + gi
                                grp = grps[gi]
                                for hh in range(2):
                                    if gg + hh >= 1:
                                        t.wait_ge(S["exp"].h, 2 * gg - 1 + hh)
                                    pss = psA if hh == 0 else psB
                                    for li, sc in enumerate(grp):
                                        mm = t.matmul(pss[:, 512 * li:512 * (li + 1)],
                                                      kw[64 * hh:64 * hh + 64, 128 * sc:128 * (sc + 1)],
                                                      qT[64 * hh:64 * hh + 64, qc0:qc0 + 512],
                                                      start=True, stop=True,
                                                      tile_position=(64 * hh, 0),
                                                      skip_group_check=True)
                                        if li == len(grp) - 1:
                                            mm.then_inc(S["tsc"].h, 1); S["tsc"].n += 1
                                if gi >= 1:
                                    if grp_masked(gi - 1):
                                        t.wait_ge(S["msk"].h, nmsk_through(gi - 1))
                                    pv_den(gi - 1, final=False)
                            t.wait_ge(S["exp"].h, 2 * (base + G))
                            if grp_masked(G - 1):
                                t.wait_ge(S["msk"].h, nmsk_through(G - 1))
                            pv_den(G - 1, final=True)
                            msk_base[0] = nmsk_through(G - 1)
                        for i in range(4):
                            for oh in range(2):
                                ch = 2 * i + oh
                                if ch == 0:
                                    t.wait_ge(S["div"].h, 8 * (k + 1))
                                if i >= 1:
                                    t.wait_ge(S["res"].h, 12 * k + 3 * (i - 1) + oh + 1)
                                pso = pvp if ch % 2 == 0 else dnp
                                for c in range(8):
                                    mm = t.matmul(pso[:, 0:512],
                                                  outT[:, 512 * c + 128 * i:512 * c + 128 * (i + 1)],
                                                  woutT_s[:, c * D + 512 * oh: c * D + 512 * (oh + 1)],
                                                  start=(c == 0), stop=(c == 7), skip_group_check=True)
                                mm.then_inc(S["op"].h, 1); S["op"].n += 1

                @blk.scalar
                def _(s):
                    for k in range(4):
                        G = NG[k]
                        grps = GRPS[k]
                        for j in range(8):
                            p = 8 * k + j
                            base = TSC_CUM[k] + j * G
                            for gi in range(G):
                                gg = base + gi
                                L = len(grps[gi])
                                s.wait_ge(S["tsc"].h, 2 * gg + 1)
                                s.activation(expb[gi % 2][0][:, 0:512 * L], psA[:, 0:512 * L],
                                             AF.Exp, scale=0.125).then_inc(S["exp"].h, 1); S["exp"].n += 1
                                s.wait_ge(S["tsc"].h, 2 * gg + 2)
                                s.activation(expb[gi % 2][1][:, 0:512 * L], psB[:, 0:512 * L],
                                             AF.Exp, scale=0.125).then_inc(S["exp"].h, 1); S["exp"].n += 1
                            s.wait_ge(S["pv"].h, p + 1)
                            s.activation(rcpa[:, :], dnp[:, :], AF.Ln)
                            s.activation(rcpb[:, :], rcpa[:, :], AF.Exp, scale=-1.0).then_inc(S["rcp"].h, 1); S["rcp"].n += 1
                        for i in range(4):
                            s.wait_ge(S["res"].h, 12 * k + 3 * i + 3)
                            s.activation(mv[:, 2:3], mv[:, 1:2], AF.Ln, bias=epsln[:, :])
                            s.drain()
                            s.activation(mv[:, 3:4], mv[:, 2:3], AF.Exp, scale=-0.5).then_inc(S["lnr"].h, 1); S["lnr"].n += 1

                @blk.vector
                def _(v):
                    for k in range(4):
                        P = SLOT_P[k]
                        grps = GRPS[k]
                        G = NG[k]
                        for j in range(8):
                            p = 8 * k + j
                            base = TSC_CUM[k] + j * G
                            for gi in range(G):
                                gg = base + gi
                                grp = grps[gi]
                                msk = [(li, sc) for li, sc in enumerate(grp)
                                       if sc >= P - 8 and MW[k][sc - (P - 8)] > 0]
                                if not msk:
                                    continue
                                for hh in range(2):
                                    v.wait_ge(S["exp"].h, 2 * gg + 1 + hh)
                                    eT = expb[gi % 2][hh]
                                    for li, sc in msk:
                                        r = sc - (P - 8)
                                        w = MW[k][r]
                                        mi = v.tensor_tensor(eT[:, 512 * li:512 * li + w],
                                                             eT[:, 512 * li:512 * li + w],
                                                             mskb[k % 2][:, 512 * r:512 * r + w],
                                                             ALU.mult)
                                    mi.then_inc(S["msk"].h, 1); S["msk"].n += 1
                            if j == 0:
                                v.wait_ge(S["op"].h, 8 * k)
                            v.wait_ge(S["rcp"].h, p + 1)
                            v.tensor_tensor(outT[:, 512 * j:512 * (j + 1)], pvp[:, :], rcpb[:, :],
                                            ALU.mult).then_inc(S["div"].h, 1); S["div"].n += 1
                            while wqst["mq"] < min(64, 2 * (p + 1)):
                                if wqst["mq"] < 32:
                                    emit_w1_quant(v)
                                else:
                                    emit_w2_group(v)
                        for i in range(4):
                            for oh in range(2):
                                v.wait_ge(S["op"].h, 8 * k + 2 * i + oh + 1)
                                v.wait_ge(S["xrl"].h, 16 * (4 * k + i + 1))
                                pso = pvp if (2 * i + oh) % 2 == 0 else dnp
                                v.tensor_tensor(xr[:, 512 * oh:512 * (oh + 1)], pso[:, 0:512],
                                                xres[:, 512 * oh:512 * (oh + 1)], ALU.add).then_inc(S["res"].h, 1); S["res"].n += 1
                            v.bn_stats(bns[:, 0:6], xr[:, 0:512])
                            v.bn_stats(bns[:, 6:12], xr[:, 512:1024])
                            v.drain()
                            v.bn_aggr(mv[:, 0:2], bns[:, :]).then_inc(S["res"].h, 1); S["res"].n += 1
                            v.wait_ge(S["lnr"].h, 4 * k + i + 1)
                            v.tensor_tensor(mv[:, 2:3], mv[:, 0:1], mv[:, 3:4], ALU.mult)
                            v.drain()
                            ti = 4 * k + i
                            if ti >= 2:
                                v.wait_ge(S["xqt"].h, 32 * (ti - 1))
                            v.tensor_scalar(x2stg[ti % 2][:, :], xr[:, :], mv[:, 3:4], mv[:, 2:3],
                                            ALU.mult, ALU.subtract)
                            v.reduce_max(x2max[:, ti:ti + 1], x2stg[ti % 2][:, :],
                                         axis=AX.X).then_inc(S["xt2"].h, 1); S["xt2"].n += 1

        pw2.close()

        with ExitStack() as p3:
            E3 = p3.enter_context
            xqT = x2T
            q32 = E3(nc.sbuf_tensor("q32", [128, D], F32))
            qi32 = E3(nc.sbuf_tensor("qi32", [128, D], I32))
            sS = E3(nc.sbuf_tensor("sS", [128, 32], F32))
            sSi = E3(nc.sbuf_tensor("sSi", [128, 32], F32))
            q32w = E3(nc.sbuf_tensor("q32w", [128, 8, 512], F32))
            qi32w = E3(nc.sbuf_tensor("qi32w", [128, 8, 512], I32))
            w1blk = [E3(nc.sbuf_tensor(f"w1blk{i}", [128, 1024], BF16)) for i in range(2)]
            hsb = [E3(nc.sbuf_tensor(f"hsb{i}", [128, TQ], BF16)) for i in range(2)]
            f1ps = [E3(nc.psum_tensor(f"f1ps{i}", [128, 512], F32)) for i in range(8)]

            with (nc.Block() if phases >= 3 else _NullBlock()) as blk:
                if phases >= 4:
                    @blk.gpsimd
                    def _(g):
                        for jf in range(2):
                            g.dma_start(out=w1blk[jf][:, :].rearrange("p (a f) -> p a f", a=8),
                                        in_=w1qt_d.ap()[jf, :, :, :]).then_inc(S["w1l"].h, 16); S["w1l"].n += 16

                @blk.vector
                def _(v):
                    v.drain()
                    v.reduce_max(red1[:, 0:1], x2max[:, :], axis=AX.X)
                    v.drain()
                    v.nop().then_inc(S["v3"].h, 1); S["v3"].n += 1

                @blk.gpsimd
                def _(g):
                    g.wait_ge(S["v3"].h, 1)
                    g.dma_start(out=bass.AP(b2row_d, 0, [[1, 128]]), in_=red1[:, 0:1]).then_inc(S["gp"].h, 16); S["gp"].n += 16
                    g.wait_ge(S["gp"].h, S["gp"].n)
                    g.dma_start(out=q32[0:1, 0:128], in_=bass.AP(b2row_d, 0, [[128, 1], [1, 128]])).then_inc(S["gp"].h, 16); S["gp"].n += 16
                    g.wait_ge(S["gp"].h, S["gp"].n)
                    g.nop().then_inc(S["gp"].h, 1); S["gp"].n += 1

                @blk.vector
                def _(v):
                    v.wait_ge(S["gp"].h, S["gp"].n)
                    v.reduce_max(tiny[0:1, 0:1], q32[0:1, 0:128], axis=AX.X)
                    v.drain()
                    v.nop().then_inc(S["v3"].h, 1); S["v3"].n += 1

                @blk.gpsimd
                def _(g):
                    g.wait_ge(S["v3"].h, 2)
                    g.dma_start(out=cc_in.ap(), in_=tiny[0:1, :]).then_inc(S["gp"].h, 16); S["gp"].n += 16
                    g.wait_ge(S["gp"].h, S["gp"].n)
                    g.collective_compute("AllReduce", ALU.max, replica_groups=[list(range(NC))],
                                         ins=[cc_in.ap().opt()], outs=[cc_out.ap().opt()]).then_inc(S["cc"].h, 1); S["cc"].n += 1
                    g.wait_ge(S["cc"].h, S["cc"].n)
                    g.dma_start(out=tiny2[0:1, :], in_=cc_out.ap()).then_inc(S["gp"].h, 16); S["gp"].n += 16

                @blk.vector
                def _(v):
                    v.wait_ge(S["gp"].h, S["gp"].n)
                    v.tensor_scalar(tiny[0:1, 0:1], tiny2[0:1, 0:1], EPS_Q, 1.0 / 255.0, ALU.max, ALU.mult)
                    v.drain()
                    v.reciprocal(tiny[0:1, 1:2], tiny[0:1, 0:1])
                    v.drain()
                    v.nop().then_inc(S["v3"].h, 1); S["v3"].n += 1

                @blk.gpsimd
                def _(g):
                    g.wait_ge(S["v3"].h, 3)
                    g.dma_start(out=bc_d.ap(), in_=tiny[0:1, :]).then_inc(S["gp"].h, 16); S["gp"].n += 16
                    g.wait_ge(S["gp"].h, S["gp"].n)
                    g.dma_start(out=scal8[:, :], in_=bass.AP(bc_d, 0, [[0, 128], [1, 8]])).then_inc(S["gp"].h, 16); S["gp"].n += 16

                @blk.vector
                def _(v):
                    v.wait_ge(S["gp"].h, S["gp"].n)
                    v.tensor_scalar(sS[:, :], sw1[:, :], scal8[:, 0:1], None, ALU.mult)
                    v.drain()
                    v.reciprocal(sSi[:, :], sS[:, :])
                    v.drain()
                    v.tensor_tensor(q32[:, 0:32], b1s[:, :], sSi[:, :], ALU.mult)
                    v.drain()
                    v.tensor_copy(qi32[:, 0:32], q32[:, 0:32])
                    v.drain()
                    v.tensor_tensor(b1qs[:, :], qi32[:, 0:32], sS[:, :], ALU.mult).then_inc(S["v3"].h, 1); S["v3"].n += 1
                    for w in range(4):
                        if w == 3:
                            v.wait_ge(S["xqt"].h, 32 * 16)
                        src_ap = x2T[:, :, 512 * w:512 * (w + 1)]
                        v.tensor_scalar(q32w[:, :, :], src_ap, scal8[:, 1:2], None, ALU.mult)
                        v.tensor_copy(qi32w[:, :, :], q32w[:, :, :])
                        v.tensor_scalar(q32w[:, :, :], qi32w[:, :, :], 0.0, 255.0, ALU.max, ALU.min)
                        v.tensor_scalar(src_ap, q32w[:, :, :], scal8[:, 0:1], None,
                                        ALU.mult).then_inc(S["xql"].h, 1); S["xql"].n += 1

                @blk.sync
                def _(sy):
                    ti = 15
                    sy.wait_ge(S["xt2"].h, ti + 1)
                    sy.dma_start_transpose(x2T[:, :, 128 * ti:128 * (ti + 1)],
                                           x2stg[ti % 2][:, :]).then_inc(S["xqt"].h, 16); S["xqt"].n += 16
                    sy.dma_start(out=x2_d.ap()[128 * ti:128 * (ti + 1), :],
                                 in_=x2stg[ti % 2][:, :]).then_inc(S["xqt"].h, 16); S["xqt"].n += 16

                if phases >= 4:
                    @blk.gpsimd
                    def _(g):
                        for jf in range(2, 32):
                            g.wait_ge(S["f1"].h, 4 * (jf - 1))
                            g.dma_start(out=w1blk[jf % 2][:, :].rearrange("p (a f) -> p a f", a=8),
                                        in_=w1qt_d.ap()[jf, :, :, :]).then_inc(S["w1l"].h, 16); S["w1l"].n += 16

                    @blk.tensor
                    def _(t):
                        for jf in range(32):
                            t.wait_ge(S["w1l"].h, 16 * (jf + 1))
                            if jf >= 2:
                                t.wait_ge(S["hev"].h, 4 * (jf - 1))
                            bs = 4 * (jf % 2)
                            for w in range(4):
                                if jf == 0:
                                    t.wait_ge(S["xql"].h, w + 1)
                                for c in range(8):
                                    mm = t.matmul(f1ps[bs + w][:, :],
                                                  w1blk[jf % 2][:, 128 * c:128 * (c + 1)],
                                                  xqT[:, c:c + 1, 512 * w:512 * (w + 1)],
                                                  start=(c == 0), stop=(c == 7))
                                mm.then_inc(S["f1"].h, 1); S["f1"].n += 1

                    @blk.vector
                    def _(v):
                        for jf in range(32):
                            if jf >= 2:
                                v.wait_ge(S["hsp"].h, 16 * (jf - 1))
                            bs = 4 * (jf % 2)
                            for w in range(4):
                                nch = 4 * jf + w
                                v.wait_ge(S["f1"].h, 4 * jf + w + 1)
                                v.tensor_scalar(hsb[jf % 2][:, 512 * w:512 * (w + 1)], f1ps[bs + w][:, :],
                                                b1qs[:, jf:jf + 1], NEG, ALU.add, ALU.max,
                                                accum_out=hstat[:, nch:nch + 1]).then_inc(S["hev"].h, 1); S["hev"].n += 1

                    @blk.sync
                    def _(sy):
                        for jf in range(32):
                            sy.wait_ge(S["hev"].h, 4 * (jf + 1))
                            sy.dma_start(out=h_d.ap()[jf, :, :], in_=hsb[jf % 2][:, :]).then_inc(S["hsp"].h, 16); S["hsp"].n += 16

            with (nc.Block() if phases >= 5 else _NullBlock()) as blk:
                @blk.vector
                def _(v):
                    v.reduce_max(red1[:, 0:1], hstat[:, :], axis=AX.X)
                    v.drain()
                    v.nop().then_inc(S["v5"].h, 1); S["v5"].n += 1

                @blk.gpsimd
                def _(g):
                    g.wait_ge(S["v5"].h, 1)
                    g.dma_start(out=bass.AP(b2row_d, 0, [[1, 128]]), in_=red1[:, 0:1]).then_inc(S["gp"].h, 16); S["gp"].n += 16
                    g.wait_ge(S["gp"].h, S["gp"].n)
                    g.dma_start(out=q32[0:1, 0:128], in_=bass.AP(b2row_d, 0, [[128, 1], [1, 128]])).then_inc(S["gp"].h, 16); S["gp"].n += 16
                    g.wait_ge(S["gp"].h, S["gp"].n)
                    g.nop().then_inc(S["gp"].h, 1); S["gp"].n += 1

                @blk.vector
                def _(v):
                    v.wait_ge(S["gp"].h, S["gp"].n)
                    v.reduce_max(tiny[0:1, 0:1], q32[0:1, 0:128], axis=AX.X)
                    v.drain()
                    v.nop().then_inc(S["v5"].h, 1); S["v5"].n += 1

                @blk.gpsimd
                def _(g):
                    g.wait_ge(S["v5"].h, 2)
                    g.dma_start(out=cc_in.ap(), in_=tiny[0:1, :]).then_inc(S["gp"].h, 16); S["gp"].n += 16
                    g.wait_ge(S["gp"].h, S["gp"].n)
                    g.collective_compute("AllReduce", ALU.max, replica_groups=[list(range(NC))],
                                         ins=[cc_in.ap().opt()], outs=[cc_out.ap().opt()]).then_inc(S["cc"].h, 1); S["cc"].n += 1
                    g.wait_ge(S["cc"].h, S["cc"].n)
                    g.dma_start(out=tiny2[0:1, :], in_=cc_out.ap()).then_inc(S["gp"].h, 16); S["gp"].n += 16

                @blk.vector
                def _(v):
                    v.wait_ge(S["gp"].h, S["gp"].n)
                    v.tensor_scalar(tiny[0:1, 2:3], tiny2[0:1, 0:1], EPS_Q, 1.0 / 255.0, ALU.max, ALU.mult)
                    v.drain()
                    v.reciprocal(tiny[0:1, 3:4], tiny[0:1, 2:3])
                    v.drain()
                    v.nop().then_inc(S["v5"].h, 1); S["v5"].n += 1

                @blk.gpsimd
                def _(g):
                    g.wait_ge(S["v5"].h, 3)
                    g.dma_start(out=bc_d.ap(), in_=tiny[0:1, :]).then_inc(S["gp"].h, 16); S["gp"].n += 16
                    g.wait_ge(S["gp"].h, S["gp"].n)
                    g.dma_start(out=scal8[:, :], in_=bass.AP(bc_d, 0, [[0, 128], [1, 8]])).then_inc(S["gp"].h, 16); S["gp"].n += 16

                @blk.vector
                def _(v):
                    v.wait_ge(S["gp"].h, S["gp"].n)
                    v.tensor_scalar(sS[:, 0:8], sw2[:, :], scal8[:, 2:3], None, ALU.mult)
                    v.drain()
                    v.reciprocal(sSi[:, 0:8], sS[:, 0:8])
                    v.drain()
                    v.tensor_tensor(q32[:, 0:8], b2s[:, :], sSi[:, 0:8], ALU.mult)
                    v.drain()
                    v.tensor_copy(qi32[:, 0:8], q32[:, 0:8])
                    v.drain()
                    v.tensor_tensor(b2qs[:, :], qi32[:, 0:8], sS[:, 0:8], ALU.mult).then_inc(S["v5"].h, 1); S["v5"].n += 1

                @blk.gpsimd
                def _(g):
                    g.wait_ge(S["v5"].h, 4)
                    g.dma_start(out=bass.AP(b2row_d, 0, [[1, 128], [128, 8]]),
                                in_=b2qs[:, :], allow_slow_non_contiguous=True).then_inc(S["gp"].h, 16); S["gp"].n += 16
                    g.wait_ge(S["gp"].h, S["gp"].n)
                    g.dma_start(out=b2qb[:, :], in_=bass.AP(b2row_d, 0, [[0, 128], [1, D]])).then_inc(S["gp"].h, 16); S["gp"].n += 16
                    g.wait_ge(S["gp"].h, S["gp"].n)
                    if debug:
                        for nm, src in [("b2qb", b2qb[:, :]), ("b1q", b1qs[:, :]),
                                        ("hstat", hstat[:, :]), ("sw1", sw1[:, :]), ("sw2", sw2[:, :]),
                                        ("s12", bc_d.ap()),
                                        ("xqT", xqT[:, :, :].rearrange("p a f -> p (a f)"))]:
                            g.dma_start(out=dbg[nm].ap(), in_=src).then_inc(S["gp"].h, 16); S["gp"].n += 16
                        g.wait_ge(S["gp"].h, S["gp"].n)

        with ExitStack() as p6:
            E6 = p6.enter_context
            hload = [E6(nc.sbuf_tensor(f"hload{i}", [128, 1024], BF16)) for i in range(4)]
            x2ld = [E6(nc.sbuf_tensor(f"x2ld{i}", [128, D], BF16)) for i in range(2)]
            hq = E6(nc.sbuf_tensor("hq", [128, 32 * 1024], BF16))
            w2blk = [E6(nc.sbuf_tensor(f"w2blk{i}", [128, 512], BF16)) for i in range(4)]
            y2sb = E6(nc.sbuf_tensor("y2sb", [128, 8 * D], F32))
            q32b = E6(nc.sbuf_tensor("q32b", [128, 1024], F32))
            qi32b = E6(nc.sbuf_tensor("qi32b", [128, 1024], I32))
            xr6 = E6(nc.sbuf_tensor("xr6", [128, D], F32))
            outt = [E6(nc.sbuf_tensor(f"outt{i}", [128, D], F32)) for i in range(2)]
            bns6 = E6(nc.sbuf_tensor("bns6", [128, 12], F32))
            mv6 = E6(nc.sbuf_tensor("mv6", [128, 4], F32))
            y2ps = [E6(nc.psum_tensor(f"y2ps{i}", [128, 512], F32)) for i in range(8)]

            EVB = [0, 8, 24, 32]

            with (nc.Block() if phases >= 6 else _NullBlock()) as blk:
                @blk.gpsimd
                def _(g):
                    for n in range(64):
                        if n >= 4:
                            g.wait_ge(S["hq"].h, n - 3)
                        tb, jf = n // 32, n % 32
                        g.dma_start(out=hload[n % 4][:, :],
                                    in_=h_d.ap()[jf, :, 1024 * tb:1024 * (tb + 1)]).then_inc(S["hl"].h, 16); S["hl"].n += 16
                    if debug:
                        for nm, src in [("qT", qT[:, :]),
                                        ("kt0", kt_d.ap()[0, :, :]), ("v0", v_d.ap()[0, :, :]),
                                        ("h0", h_d.ap()[0, :, :])]:
                            g.dma_start(out=dbg[nm].ap(), in_=src).then_inc(S["gp"].h, 16); S["gp"].n += 16
                        g.dma_start(out=dbg["x2"].ap().rearrange("p (a d) -> p a d", a=16),
                                    in_=x2_d.ap().rearrange("(a p) d -> p a d", p=128)).then_inc(S["gp"].h, 16); S["gp"].n += 16
                        g.wait_ge(S["gp"].h, S["gp"].n)

                @blk.sync
                def _(sy):
                    for tb in range(2):
                        for oh in range(2):
                            for jf in range(32):
                                n = 64 * tb + 32 * oh + jf
                                if n >= 4:
                                    sy.wait_ge(S["f2"].h, n - 3)
                                sy.dma_start(out=w2blk[n % 4][:, :],
                                             in_=w2qt_d.ap()[jf, :, 512 * oh:512 * (oh + 1)]).then_inc(S["w2l"].h, 16); S["w2l"].n += 16
                        for i in range(8):
                            ti = 8 * tb + i
                            sy.wait_ge(S["ot"].h, ti + 1)
                            sy.dma_start(out=out.ap()[128 * ti:128 * (ti + 1), :],
                                         in_=outt[ti % 2][:, :]).then_inc(S["o"].h, 16); S["o"].n += 16
                    sy.wait_ge(S["o"].h, S["o"].n)

                @blk.vector
                def _(v):
                    def hquant(n):
                        jf = n % 32
                        v.wait_ge(S["hl"].h, 16 * (n + 1))
                        if n >= 32:
                            v.wait_ge(S["f2"].h, 32 + jf + 1)
                        v.tensor_scalar(q32b[:, :], hload[n % 4][:, :], scal8[:, 3:4], None, ALU.mult)
                        v.tensor_copy(qi32b[:, :], q32b[:, :])
                        v.tensor_scalar(q32b[:, :], qi32b[:, :], 0.0, 255.0, ALU.max, ALU.min)
                        v.tensor_scalar(hq[:, 1024 * jf:1024 * (jf + 1)], q32b[:, :], scal8[:, 2:3], None,
                                        ALU.mult).then_inc(S["hq"].h, 1); S["hq"].n += 1

                    for tb in range(2):
                        if tb == 0:
                            for n in range(32):
                                hquant(n)
                        for oh in range(2):
                            if tb == 0 and oh == 1:
                                for n in range(32, 56):
                                    hquant(n)
                            v.wait_ge(S["f2"].h, 32 * (2 * tb + oh + 1))
                            for i in range(8):
                                v.tensor_tensor(y2sb[:, D * i + 512 * oh:D * i + 512 * (oh + 1)],
                                                y2ps[i][:, :], b2qb[:, 512 * oh:512 * (oh + 1)],
                                                ALU.add).then_inc(S["y2e"].h, 1); S["y2e"].n += 1
                            if tb == 0 and oh == 1:
                                for n in range(56, 64):
                                    hquant(n)
                        for i in range(8):
                            ti = 8 * tb + i
                            v.wait_ge(S["x2l"].h, 16 * (ti + 1))
                            v.tensor_tensor(xr6[:, :], y2sb[:, D * i:D * (i + 1)],
                                            x2ld[ti % 2][:, :], ALU.add)
                            v.bn_stats(bns6[:, 0:6], xr6[:, 0:512])
                            v.bn_stats(bns6[:, 6:12], xr6[:, 512:1024])
                            v.drain()
                            v.bn_aggr(mv6[:, 0:2], bns6[:, :]).then_inc(S["y2e"].h, 1); S["y2e"].n += 1
                            v.wait_ge(S["ln2"].h, ti + 1)
                            v.tensor_tensor(mv6[:, 2:3], mv6[:, 0:1], mv6[:, 3:4], ALU.mult)
                            v.drain()
                            if ti >= 2:
                                v.wait_ge(S["o"].h, 16 * (ti - 1))
                            v.tensor_scalar(outt[ti % 2][:, :], xr6[:, :], mv6[:, 3:4], mv6[:, 2:3],
                                            ALU.mult, ALU.subtract).then_inc(S["ot"].h, 1); S["ot"].n += 1

                @blk.tensor
                def _(t):
                    for tb in range(2):
                        for oh in range(2):
                            sec = 2 * tb + oh
                            for jf in range(32):
                                n = 64 * tb + 32 * oh + jf
                                t.wait_ge(S["hq"].h, 32 * tb + jf + 1)
                                t.wait_ge(S["w2l"].h, 16 * (n + 1))
                                for i in range(8):
                                    if jf == 0 and sec >= 1:
                                        t.wait_ge(S["y2e"].h, EVB[sec - 1] + i + 1)
                                    mm = t.matmul(y2ps[i][:, :],
                                                  hq[:, 1024 * jf + 128 * i:1024 * jf + 128 * (i + 1)],
                                                  w2blk[n % 4][:, :],
                                                  start=(jf == 0), stop=(jf == 31), skip_group_check=True)
                                mm.then_inc(S["f2"].h, 1); S["f2"].n += 1

                @blk.scalar
                def _(s):
                    for ti in range(2):
                        s.dma_start(out=x2ld[ti % 2][:, :],
                                    in_=x2_d.ap()[128 * ti:128 * (ti + 1), :]).then_inc(S["x2l"].h, 16); S["x2l"].n += 16
                    for tb in range(2):
                        for i in range(8):
                            ti = 8 * tb + i
                            s.wait_ge(S["y2e"].h, 24 * tb + 16 + i + 1)
                            s.activation(mv6[:, 2:3], mv6[:, 1:2], AF.Ln, bias=epsln[:, :])
                            s.drain()
                            s.activation(mv6[:, 3:4], mv6[:, 2:3], AF.Exp, scale=-0.5).then_inc(S["ln2"].h, 1); S["ln2"].n += 1
                            if ti + 2 < 16:
                                s.wait_ge(S["ot"].h, ti + 1)
                                s.dma_start(out=x2ld[ti % 2][:, :],
                                            in_=x2_d.ap()[128 * (ti + 2):128 * (ti + 3), :]).then_inc(S["x2l"].h, 16); S["x2l"].n += 16

        if debug and phases < 6:
            with nc.Block() as blk:
                @blk.gpsimd
                def _(g):
                    pairs = [("kt0", kt_d.ap()[0, :, :]), ("v0", v_d.ap()[0, :, :]),
                             ("qT", qT[:, :])]
                    pass
                    if phases >= 3:
                        pairs.append(("s12", bc_d.ap()))
                        pairs.append(("b1q", b1qs[:, :]))
                        pairs.append(("sw1", sw1[:, :]))
                    if phases >= 4:
                        pairs.append(("h0", h_d.ap()[0, :, :]))
                        pairs.append(("hstat", hstat[:, :]))
                    for nm, srcp in pairs:
                        g.dma_start(out=dbg[nm].ap(), in_=srcp).then_inc(S["gp"].h, 16); S["gp"].n += 16
                    g.wait_ge(S["gp"].h, S["gp"].n)

    return nc, dbg


def _host_masks(parity):
    m = np.zeros((4, 8, 128, 512), dtype=np.float32)
    for k in range(4):
        P = SLOT_P[k]
        T0 = 512 * QCH[parity][k]
        for r in range(8):
            S0 = 128 * (P - 8 + r)
            srow = S0 + np.arange(128)[:, None]
            tcol = T0 + np.arange(512)[None, :]
            m[k, r] = np.where(srow > tcol, 0.0, 1.0)
    return np.ascontiguousarray(m.astype(ml_dtypes.bfloat16))


_CACHE = {}


def kernel(L0_attn_x, L0_attn_W_qkv, L0_attn_W_out, L0_ffn_w1, L0_ffn_b1, L0_ffn_w2, L0_ffn_b2,
           debug=False):
    x = np.ascontiguousarray(np.asarray(L0_attn_x, dtype=np.float32))
    wqkvT_h = np.ascontiguousarray(np.asarray(L0_attn_W_qkv, dtype=np.float32).T)
    woutT_h = np.ascontiguousarray(np.asarray(L0_attn_W_out, dtype=np.float32).T)
    w1_h = np.ascontiguousarray(np.asarray(L0_ffn_w1, dtype=np.float32))
    b1_h = np.ascontiguousarray(np.asarray(L0_ffn_b1, dtype=np.float32).reshape(32, 128).T)
    w2_h = np.ascontiguousarray(np.asarray(L0_ffn_w2, dtype=np.float32))
    b2_h = np.ascontiguousarray(np.asarray(L0_ffn_b2, dtype=np.float32).reshape(8, 128).T)
    ident = np.ascontiguousarray(np.eye(128, dtype=np.float32).astype(ml_dtypes.bfloat16))

    key = ("prog", bool(debug), PHASES)
    if key not in _CACHE:
        _CACHE[key] = build_program(debug=bool(debug))
    nc, _ = _CACHE[key]

    in_maps = []
    for c in range(NC):
        b, parity = c // 2, c % 2
        xb = np.ascontiguousarray(x[b])
        rows = np.ascontiguousarray(
            np.concatenate([xb[512 * cq:512 * (cq + 1)] for cq in QCH[parity]], axis=0))
        in_maps.append({
            "x_kv": xb, "x_own": rows,
            "wqkvT": wqkvT_h, "woutT": woutT_h,
            "w1": w1_h, "b1": b1_h, "w2": w2_h, "b2": b2_h,
            "masks": _host_masks(parity), "ident_bf": ident,
        })
    res = run_bass_kernel_spmd(nc, in_maps, core_ids=list(range(NC)))
    outp = np.empty((B, T, D), dtype=np.float32)
    for c in range(NC):
        b, parity = c // 2, c % 2
        o = res.results[c]["out"]
        for k, cq in enumerate(QCH[parity]):
            outp[b, 512 * cq:512 * (cq + 1)] = o[512 * k:512 * (k + 1)]
    if debug:
        return outp, res
    return outp


# version 25
# speedup vs baseline: 1.0530x; 1.0204x over previous
import numpy as np
import ml_dtypes
from contextlib import ExitStack

import concourse.bass as bass
import concourse.mybir as mybir
from concourse.bass_utils import run_bass_kernel_spmd

F32 = mybir.dt.float32
BF16 = mybir.dt.bfloat16
I32 = mybir.dt.int32
AF = mybir.ActivationFunctionType
ALU = mybir.AluOpType
AX = mybir.AxisListType

B, T, D = 4, 4096, 1024
FF = 4096
NC = 8
TQ = 2048
SLOT_P = [8, 16, 24, 32]
QCH = {0: [0, 3, 4, 7], 1: [1, 2, 5, 6]}
EPS_Q = 1e-8
EPS_LN = 1e-5
NEG = -1e30


def groups_of(P):
    G = (P + 2) // 3
    sizes = [3] * (P - 2 * G) + [2] * (3 * G - P)
    out, s = [], 0
    for sz in sizes:
        out.append(list(range(s, s + sz)))
        s += sz
    assert s == P
    return out


GRPS = [groups_of(P) for P in SLOT_P]
NG = [len(g) for g in GRPS]
TSC_CUM = [0]
for _k in range(4):
    TSC_CUM.append(TSC_CUM[-1] + 8 * NG[_k])


def _mask_width(k, r):
    P = SLOT_P[k]
    c = P - 8 + r
    w = 0
    for parity in (0, 1):
        cq = QCH[parity][k]
        E = 4 * (cq + 1)
        if c >= E:
            w = 512
        else:
            d = 512 * cq - 128 * c
            w = max(w, max(0, min(512, 128 - d)))
    return w



MW = [[_mask_width(k, r) for r in range(8)] for k in range(4)]


class _NullEng:
    def __call__(self, f):
        return None


class _NullBlockInner:
    def __getattr__(self, name):
        return _NullEng()


class _NullBlock:
    def __enter__(self):
        return _NullBlockInner()

    def __exit__(self, *a):
        return False


class Sem:
    def __init__(self, ctx, nc, name):
        self.h = ctx.enter_context(nc.semaphore(name))
        self.n = 0


PHASES = 6


def build_program(debug=False, phases=None):
    phases = PHASES if phases is None else phases
    nc = bass.Bass("TRN2", target_bir_lowering=False)

    x_kv = nc.dram_tensor("x_kv", [T, D], F32, kind="ExternalInput")
    x_own = nc.dram_tensor("x_own", [TQ, D], F32, kind="ExternalInput")
    wqkvT = nc.dram_tensor("wqkvT", [D, 3 * D], F32, kind="ExternalInput")
    woutT = nc.dram_tensor("woutT", [D, D], F32, kind="ExternalInput")
    w1 = nc.dram_tensor("w1", [FF, D], F32, kind="ExternalInput")
    b1 = nc.dram_tensor("b1", [128, 32], F32, kind="ExternalInput")
    w2 = nc.dram_tensor("w2", [D, FF], F32, kind="ExternalInput")
    b2 = nc.dram_tensor("b2", [128, 8], F32, kind="ExternalInput")
    masks = nc.dram_tensor("masks", [4, 8, 128, 512], BF16, kind="ExternalInput")
    ident_bf = nc.dram_tensor("ident_bf", [128, 128], BF16, kind="ExternalInput")
    out = nc.dram_tensor("out", [TQ, D], F32, kind="ExternalOutput")

    kt_d = nc.dram_tensor("kt_d", [8, 128, T], BF16)
    v_d = nc.dram_tensor("v_d", [32, 128, D], BF16)
    w1qt_d = nc.dram_tensor("w1qt_d", [32, 128, 8, 128], BF16)
    w2qt_d = nc.dram_tensor("w2qt_d", [32, 128, D], BF16)
    h_d = nc.dram_tensor("h_d", [32, 128, TQ], BF16)
    x2_d = nc.dram_tensor("x2_d", [TQ, D], BF16)
    cc_in = nc.dram_tensor("cc_in", [1, 8], F32)
    cc_out = nc.dram_tensor("cc_out", [1, 8], F32)
    bc_d = nc.dram_tensor("bc_d", [1, 8], F32)
    b2row_d = nc.dram_tensor("b2row_d", [1, D], F32)

    dbg = {}
    if debug:
        for nm, shp in [("qT", [128, 8 * TQ]), ("x2", [128, 16 * D]),
                        ("xqT", [128, 8 * TQ]), ("s12", [1, 8]),
                        ("kt0", [128, T]), ("v0", [128, D]), ("h0", [128, TQ]),
                        ("hstat", [128, 128]), ("b1q", [128, 32]), ("b2qb", [128, D]),
                        ("sw1", [128, 32]), ("sw2", [128, 8])]:
            dbg[nm] = nc.dram_tensor("dbg_" + nm, shp, F32, kind="ExternalOutput")

    ctx = ExitStack()
    with ctx:
        E = ctx.enter_context
        qT = E(nc.sbuf_tensor("qT", [128, 8 * TQ], BF16))
        x2T = E(nc.sbuf_tensor("x2T", [128, 8, TQ], BF16))
        sw1 = E(nc.sbuf_tensor("sw1", [128, 32], F32))
        sw2 = E(nc.sbuf_tensor("sw2", [128, 8], F32))
        b1s = E(nc.sbuf_tensor("b1s", [128, 32], F32))
        b2s = E(nc.sbuf_tensor("b2s", [128, 8], F32))
        b1qs = E(nc.sbuf_tensor("b1qs", [128, 32], F32))
        b2qs = E(nc.sbuf_tensor("b2qs", [128, 8], F32))
        x2max = E(nc.sbuf_tensor("x2max", [128, 16], F32))
        hstat = E(nc.sbuf_tensor("hstat", [128, 128], F32))
        onesb = E(nc.sbuf_tensor("onesb", [128, 64], BF16))
        identb = E(nc.sbuf_tensor("identb", [128, 128], BF16))
        scal8 = E(nc.sbuf_tensor("scal8", [128, 8], F32))
        tiny = E(nc.sbuf_tensor("tiny", [1, 8], F32))
        tiny2 = E(nc.sbuf_tensor("tiny2", [1, 8], F32))
        b2qb = E(nc.sbuf_tensor("b2qb", [128, D], F32))
        red1 = E(nc.sbuf_tensor("red1", [128, 4], F32))
        epsln = E(nc.sbuf_tensor("epsln", [128, 1], F32))
        pw2 = ExitStack()
        woutT_s = pw2.enter_context(nc.sbuf_tensor("woutT_s", [128, 8 * D], BF16))
        mskb = [pw2.enter_context(nc.sbuf_tensor(f"mskb{i}", [128, 8 * 512], BF16)) for i in range(2)]

        S = {}
        for nm in ["ldx", "trx", "ldo", "tro", "ldc", "mm", "ev", "sp",
                   "ktl", "vl", "mkl", "xrl", "tsc", "exp", "pv", "rcp", "div",
                   "op", "res", "lnr", "xt2",
                   "wl", "qv", "wtr", "wsp", "msk",
                   "gp", "cc", "v3", "xql", "xqt",
                   "w1l", "f1", "hev", "hsp", "v5",
                   "hl", "hq", "w2l", "f2", "y2e", "ln2", "ot", "o", "x2l"]:
            S[nm] = Sem(ctx, nc, "s_" + nm)

        def _tail_dump():
            if not debug:
                return
            with nc.Block() as blk:
                @blk.gpsimd
                def _(g):
                    pairs = [("kt0", kt_d.ap()[0, :, :]), ("v0", v_d.ap()[0, :, :]),
                             ("qT", qT[:, :])]
                    pass
                    if phases >= 3:
                        pairs.append(("s12", bc_d.ap()))
                        pairs.append(("b1q", b1qs[:, :]))
                    if phases >= 4:
                        pairs.append(("h0", h_d.ap()[0, :, :]))
                        pairs.append(("hstat", hstat[:, :]))
                    for nm, srcp in pairs:
                        g.dma_start(out=dbg[nm].ap(), in_=srcp).then_inc(S["gp"].h, 16); S["gp"].n += 16
                    g.wait_ge(S["gp"].h, S["gp"].n)

        wqst = {"mload": 0, "nwq": 0, "mq": 0}

        def wload_src(m):
            if m < 32:
                return w1.ap()[128 * m:128 * (m + 1), :]
            q = m - 32
            jj, qq = q // 4, q % 4
            return w2.ap()[128 * jj:128 * (jj + 1), 1024 * qq:1024 * (qq + 1)]

        def emit_wload(eng):
            m = wqst["mload"]
            if m >= 4:
                eng.wait_ge(S["qv"].h, m - 3)
            eng.dma_start(out=wstg[m % 4][:, :], in_=wload_src(m)).then_inc(S["wl"].h, 16); S["wl"].n += 16
            wqst["mload"] = m + 1

        def emit_wq_step(sy):
            nwq = wqst["nwq"]
            sy.wait_ge(S["qv"].h, nwq + 1)
            if nwq >= 2:
                sy.wait_ge(S["wsp"].h, 16 * (nwq - 1))
            sy.dma_start_transpose(wtstg[nwq % 2][:, :, :],
                                   wqstg[nwq % 4][:, :]).then_inc(S["wtr"].h, 16); S["wtr"].n += 16
            sy.wait_ge(S["wtr"].h, S["wtr"].n)
            if nwq < 32:
                dst = w1qt_d.ap()[nwq, :, :, :]
            else:
                q = nwq - 32
                jj, qq = q // 4, q % 4
                dst = w2qt_d.ap()[8 * qq:8 * qq + 8, :, 128 * jj:128 * (jj + 1)].rearrange("a p f -> p a f")
            sy.dma_start(out=dst, in_=wtstg[nwq % 2][:, :, :]).then_inc(S["wsp"].h, 16); S["wsp"].n += 16
            wqst["nwq"] = nwq + 1

        def emit_w1_quant(v):
            j = wqst["mq"]
            v.wait_ge(S["wl"].h, 16 * (j + 1))
            if j >= 4:
                v.wait_ge(S["wtr"].h, 16 * (j - 3))
            src_t = wstg[j % 4]
            v.drain()
            v.reduce_max(red1[:, 0:1], src_t[:, :], axis=AX.X, apply_absolute_value=True)
            v.drain()
            v.tensor_scalar(sw1[:, j:j + 1], red1[:, 0:1], EPS_Q, 1.0 / 127.0, ALU.max, ALU.mult)
            v.drain()
            v.reciprocal(red1[:, 1:2], sw1[:, j:j + 1])
            v.drain()
            v.tensor_scalar(src_t[:, :], src_t[:, :], red1[:, 1:2], None, ALU.mult)
            v.tensor_copy(wint[:, :], src_t[:, :])
            v.tensor_scalar(wqstg[j % 4][:, :], wint[:, :], sw1[:, j:j + 1], None,
                            ALU.mult).then_inc(S["qv"].h, 1); S["qv"].n += 1
            wqst["mq"] = j + 1

        def emit_w2_group(v):
            q = wqst["mq"] - 32
            jj = q // 4
            v.wait_ge(S["wl"].h, 16 * (32 + 4 * jj + 4))
            if jj >= 1:
                v.wait_ge(S["wtr"].h, 16 * (32 + 4 * (jj - 1) + 4))
            v.drain()
            for qq in range(4):
                v.reduce_max(red1[:, qq:qq + 1], wstg[qq][:, :], axis=AX.X, apply_absolute_value=True)
            v.drain()
            v.tensor_tensor(red1[:, 0:1], red1[:, 0:1], red1[:, 1:2], ALU.max)
            v.tensor_tensor(red1[:, 2:3], red1[:, 2:3], red1[:, 3:4], ALU.max)
            v.drain()
            v.tensor_tensor(red1[:, 0:1], red1[:, 0:1], red1[:, 2:3], ALU.max)
            v.drain()
            v.tensor_scalar(sw2[:, jj:jj + 1], red1[:, 0:1], EPS_Q, 1.0 / 127.0, ALU.max, ALU.mult)
            v.drain()
            v.reciprocal(red1[:, 1:2], sw2[:, jj:jj + 1])
            v.drain()
            for qq in range(4):
                v.tensor_scalar(wstg[qq][:, :], wstg[qq][:, :], red1[:, 1:2], None, ALU.mult)
                v.tensor_copy(wint[:, :], wstg[qq][:, :])
                v.tensor_scalar(wqstg[(32 + 4 * jj + qq) % 4][:, :], wint[:, :], sw2[:, jj:jj + 1], None,
                                ALU.mult).then_inc(S["qv"].h, 1); S["qv"].n += 1
            wqst["mq"] += 4

        with ExitStack() as p1:
            E1 = p1.enter_context
            wqkvT_s = E1(nc.sbuf_tensor("wqkvT_s", [128, 8 * 3 * D], BF16))
            xtw = [E1(nc.sbuf_tensor(f"xtw{i}", [128, 8, 512], BF16)) for i in range(2)]
            xow = [E1(nc.sbuf_tensor(f"xow{i}", [128, 8, 512], BF16)) for i in range(3)]
            xstg = [E1(nc.sbuf_tensor(f"xstg{i}", [128, D], BF16)) for i in range(2)]
            xostg = [E1(nc.sbuf_tensor(f"xostg{i}", [128, D], BF16)) for i in range(2)]
            kvstg = [E1(nc.sbuf_tensor(f"kvstg{i}", [128, 512], BF16)) for i in range(8)]
            ps = [E1(nc.psum_tensor(f"p1ps{i}", [128, 512], F32)) for i in range(4)]

            with (nc.Block() if phases >= 1 else _NullBlock()) as blk:
                @blk.gpsimd
                def _(g):
                    g.dma_start(out=identb[:, :], in_=ident_bf.ap()).then_inc(S["ldc"].h, 16); S["ldc"].n += 16
                    g.dma_start(out=wqkvT_s[:, :].rearrange("p (a f) -> p a f", a=8),
                                in_=wqkvT.ap().rearrange("(a p) f -> p a f", p=128)).then_inc(S["ldc"].h, 16); S["ldc"].n += 16

                    def spills_for_window(w):
                        base = 16 * w
                        for u in range(16):
                            nsp = base + u + 1
                            g.wait_ge(S["ev"].h, nsp)
                            if u < 8:
                                g.dma_start(out=kt_d.ap()[u, :, 512 * w:512 * (w + 1)],
                                            in_=kvstg[(nsp - 1) % 8][:, :]).then_inc(S["sp"].h, 16)
                            else:
                                i, fh = (u - 8) // 2, (u - 8) % 2
                                g.dma_start(out=v_d.ap()[4 * w + i, :, 512 * fh:512 * (fh + 1)],
                                            in_=kvstg[(nsp - 1) % 8][:, :]).then_inc(S["sp"].h, 16)
                            S["sp"].n += 16

                    io = 0
                    for i in range(32):
                        if i >= 2:
                            g.wait_ge(S["trx"].h, 16 * (i - 1))
                        g.dma_start(out=xstg[i % 2][:, :],
                                    in_=x_kv.ap()[128 * i:128 * (i + 1), :]).then_inc(S["ldx"].h, 16); S["ldx"].n += 16
                        if i % 2 == 1 and io < 14:
                            if io >= 2:
                                g.wait_ge(S["tro"].h, 16 * (io - 1))
                            g.dma_start(out=xostg[io % 2][:, :],
                                        in_=x_own.ap()[128 * io:128 * (io + 1), :]).then_inc(S["ldo"].h, 16); S["ldo"].n += 16
                            io += 1
                        if i % 4 == 3 and i // 4 >= 1:
                            spills_for_window(i // 4 - 1)
                    spills_for_window(7)
                    g.dma_start(out=woutT_s[:, :].rearrange("p (a f) -> p a f", a=8),
                                in_=woutT.ap().rearrange("(a p) f -> p a f", p=128)).then_inc(S["ldc"].h, 16); S["ldc"].n += 16
                    g.dma_start(out=mskb[0][:, :].rearrange("p (a f) -> p a f", a=8),
                                in_=masks.ap()[0, :, :, :].rearrange("a p f -> p a f")).then_inc(S["mkl"].h, 16); S["mkl"].n += 16
                    for io in range(14, 16):
                        g.wait_ge(S["tro"].h, 16 * (io - 1))
                        g.dma_start(out=xostg[io % 2][:, :],
                                    in_=x_own.ap()[128 * io:128 * (io + 1), :]).then_inc(S["ldo"].h, 16); S["ldo"].n += 16

                @blk.sync
                def _(sy):
                    io = 0
                    for i in range(32):
                        sy.wait_ge(S["ldx"].h, 16 * (i + 1))
                        w, sl = i // 4, i % 4
                        if sl == 0 and w >= 2:
                            sy.wait_ge(S["mm"].h, 16 * (w - 1))
                        sy.dma_start_transpose(xtw[w % 2][:, :, 128 * sl:128 * (sl + 1)],
                                               xstg[i % 2][:, :]).then_inc(S["trx"].h, 16); S["trx"].n += 16
                        if i % 2 == 1 and io < 12:
                            sy.wait_ge(S["ldo"].h, 16 * (io + 1))
                            w2_, sl2 = io // 4, io % 4
                            sy.dma_start_transpose(xow[w2_ % 3][:, :, 128 * sl2:128 * (sl2 + 1)],
                                                   xostg[io % 2][:, :]).then_inc(S["tro"].h, 16); S["tro"].n += 16
                            io += 1
                    sy.wait_ge(S["mm"].h, 136)
                    for io in range(12, 16):
                        sy.wait_ge(S["ldo"].h, 16 * (io + 1))
                        sy.dma_start_transpose(xow[0][:, :, 128 * (io % 4):128 * (io % 4 + 1)],
                                               xostg[io % 2][:, :]).then_inc(S["tro"].h, 16); S["tro"].n += 16
                    sy.dma_start(out=b1s[:, :], in_=b1.ap()).then_inc(S["ldc"].h, 16); S["ldc"].n += 16
                    sy.dma_start(out=b2s[:, :], in_=b2.ap()).then_inc(S["ldc"].h, 16); S["ldc"].n += 16

                @blk.scalar
                def _(s):
                    nev = 0
                    for w in range(8):
                        for u in range(16):
                            nev += 1
                            s.wait_ge(S["mm"].h, nev)
                            if nev > 8:
                                s.wait_ge(S["sp"].h, 16 * (nev - 8))
                            s.copy(kvstg[(nev - 1) % 8][:, :], ps[(nev - 1) % 4][:, :]).then_inc(S["ev"].h, 1); S["ev"].n += 1
                    for w in range(4):
                        for f in range(8):
                            nev += 1
                            s.wait_ge(S["mm"].h, nev)
                            s.copy(qT[:, f * TQ + 512 * w: f * TQ + 512 * (w + 1)],
                                   ps[(nev - 1) % 4][:, :]).then_inc(S["ev"].h, 1); S["ev"].n += 1

                @blk.tensor
                def _(t):
                    t.wait_ge(S["ldc"].h, 32)
                    nmm = 0
                    for w in range(8):
                        t.wait_ge(S["trx"].h, 16 * 4 * (w + 1))
                        xw = xtw[w % 2]
                        for u in range(16):
                            nmm += 1
                            if nmm > 4:
                                t.wait_ge(S["ev"].h, nmm - 4)
                            for c in range(8):
                                last = (c == 7)
                                if u < 8:
                                    mm = t.matmul(ps[(nmm - 1) % 4][:, :],
                                                  wqkvT_s[:, c * 3 * D + D + 128 * u: c * 3 * D + D + 128 * (u + 1)],
                                                  xw[:, c:c + 1, :],
                                                  start=(c == 0), stop=last)
                                else:
                                    i, fh = (u - 8) // 2, (u - 8) % 2
                                    mm = t.matmul(ps[(nmm - 1) % 4][:, :],
                                                  xw[:, c:c + 1, 128 * i:128 * (i + 1)],
                                                  wqkvT_s[:, c * 3 * D + 2 * D + 512 * fh: c * 3 * D + 2 * D + 512 * (fh + 1)],
                                                  start=(c == 0), stop=last)
                                if last:
                                    mm.then_inc(S["mm"].h, 1); S["mm"].n += 1
                    for w in range(4):
                        t.wait_ge(S["tro"].h, 16 * 4 * (w + 1))
                        xw = xow[w % 3]
                        for f in range(8):
                            nmm += 1
                            if nmm > 4:
                                t.wait_ge(S["ev"].h, nmm - 4)
                            for c in range(8):
                                mm = t.matmul(ps[(nmm - 1) % 4][:, :],
                                              wqkvT_s[:, c * 3 * D + 128 * f: c * 3 * D + 128 * (f + 1)],
                                              xw[:, c:c + 1, :],
                                              start=(c == 0), stop=(c == 7))
                                if c == 7:
                                    mm.then_inc(S["mm"].h, 1); S["mm"].n += 1

                @blk.vector
                def _(v):
                    v.memset(onesb[:, :], 1.0)
                    v.memset(epsln[:, :], EPS_LN)
                    v.memset(x2max[:, :], NEG)
                    v.memset(tiny[:, :], NEG)
                    v.memset(tiny2[:, :], NEG)

        with ExitStack() as p2:
            E2 = p2.enter_context
            ktw = [E2(nc.sbuf_tensor(f"ktw{i}", [128, 32 * 128], BF16)) for i in range(2)]
            vw = [E2(nc.sbuf_tensor(f"vw{i}", [128, 32 * 128], BF16)) for i in range(2)]
            wstg = [E2(nc.sbuf_tensor(f"wstg{i}", [128, 1024], F32)) for i in range(4)]
            wqstg = [E2(nc.sbuf_tensor(f"wqstg{i}", [128, 1024], BF16)) for i in range(4)]
            wtstg = [E2(nc.sbuf_tensor(f"wtstg{i}", [128, 8, 128], BF16)) for i in range(2)]
            wint = E2(nc.sbuf_tensor("wint", [128, 1024], I32))
            expb = [[E2(nc.sbuf_tensor(f"expb{i}{hh}", [128, 1536], BF16)) for hh in range(2)] for i in range(2)]
            outT = E2(nc.sbuf_tensor("outT", [128, 8 * 512], BF16))
            rcpa = E2(nc.sbuf_tensor("rcpa", [128, 512], F32))
            rcpb = E2(nc.sbuf_tensor("rcpb", [128, 512], F32))
            xres = E2(nc.sbuf_tensor("xres", [128, D], F32))
            xr = E2(nc.sbuf_tensor("xr", [128, D], F32))
            x2stg = [E2(nc.sbuf_tensor(f"x2stg{i}", [128, D], BF16)) for i in range(2)]
            bns = E2(nc.sbuf_tensor("bns", [128, 12], F32))
            mv = E2(nc.sbuf_tensor("mv", [128, 4], F32))
            psA = E2(nc.psum_tensor("psA", [128, 1536], F32))
            psB = E2(nc.psum_tensor("psB", [128, 1536], F32))
            pvp = E2(nc.psum_tensor("pvp", [128, 512], F32))
            dnp = E2(nc.psum_tensor("dnp", [128, 512], F32))

            with (nc.Block() if phases >= 2 else _NullBlock()) as blk:
                @blk.gpsimd
                def _(g):
                    for p in range(32):
                        k, j = p // 8, p % 8
                        P = SLOT_P[k]
                        if p >= 2:
                            g.wait_ge(S["pv"].h, p - 1)
                        g.dma_start(out=ktw[p % 2][:, 0:128 * P],
                                    in_=kt_d.ap()[j, :, 0:128 * P]).then_inc(S["ktl"].h, 16); S["ktl"].n += 16
                        g.dma_start(out=vw[p % 2][:, 0:128 * P].rearrange("p (a f) -> p a f", a=P),
                                    in_=v_d.ap()[0:P, :, 128 * j:128 * (j + 1)].rearrange("a p f -> p a f")
                                    ).then_inc(S["vl"].h, 16); S["vl"].n += 16
                        if j == 1:
                            if k >= 1:
                                g.wait_ge(S["res"].h, 12 * k)
                            g.dma_start(out=xres[:, :],
                                        in_=x_own.ap()[512 * k:512 * k + 128, :]).then_inc(S["xrl"].h, 16); S["xrl"].n += 16
                        for _ in range(2):
                            if wqst["mload"] < 64:
                                emit_wload(g)

                @blk.sync
                def _(sy):
                    for k in range(4):
                        if k >= 1:
                            if k >= 2:
                                sy.wait_ge(S["tsc"].h, 2 * TSC_CUM[k - 1])
                            sy.dma_start(out=mskb[k % 2][:, :].rearrange("p (a f) -> p a f", a=8),
                                         in_=masks.ap()[k, :, :, :].rearrange("a p f -> p a f")).then_inc(S["mkl"].h, 16); S["mkl"].n += 16
                        for _ in range(16):
                            if wqst["nwq"] < 64:
                                emit_wq_step(sy)
                        for i in range(4):
                            n = 4 * k + i
                            if i >= 1:
                                sy.wait_ge(S["res"].h, 3 * n)
                                sy.dma_start(out=xres[:, :],
                                             in_=x_own.ap()[512 * k + 128 * i:512 * k + 128 * (i + 1), :]).then_inc(S["xrl"].h, 16); S["xrl"].n += 16
                            if n >= 1:
                                ti = n - 1
                                sy.wait_ge(S["xt2"].h, ti + 1)
                                sy.dma_start_transpose(x2T[:, :, 128 * ti:128 * (ti + 1)],
                                                       x2stg[ti % 2][:, :]).then_inc(S["xqt"].h, 16); S["xqt"].n += 16
                                sy.dma_start(out=x2_d.ap()[128 * ti:128 * (ti + 1), :],
                                             in_=x2stg[ti % 2][:, :]).then_inc(S["xqt"].h, 16); S["xqt"].n += 16

                @blk.tensor
                def _(t):
                    t.wait_ge(S["ldc"].h, S["ldc"].n)
                    msk_base = [0]
                    for k in range(4):
                        P = SLOT_P[k]
                        grps = GRPS[k]
                        G = NG[k]
                        t.wait_ge(S["mkl"].h, 16 * (k + 1))
                        for j in range(8):
                            p = 8 * k + j
                            t.wait_ge(S["ktl"].h, 16 * (p + 1))
                            t.wait_ge(S["vl"].h, 16 * (p + 1))
                            kw, vv = ktw[p % 2], vw[p % 2]
                            qc0 = j * TQ + 512 * k
                            base = TSC_CUM[k] + j * G
                            gate = []
                            if p >= 1:
                                gate.append((S["div"].h, p))
                            if j == 0 and k >= 1:
                                gate.append((S["res"].h, 12 * k - 1))
                            pv_gate = [gate if gate else None]

                            def pv_den(gi, final):
                                if pv_gate[0] is not None:
                                    for sem, val in pv_gate[0]:
                                        t.wait_ge(sem, val)
                                    pv_gate[0] = None
                                grp = grps[gi]
                                eA, eB = expb[gi % 2]
                                for li, sc in enumerate(grp):
                                    st0 = (sc == 0)
                                    fin = final and (li == len(grp) - 1)
                                    t.matmul(pvp[0:64, :], vv[:, 128 * sc:128 * sc + 64],
                                             eA[:, 512 * li:512 * (li + 1)],
                                             start=st0, stop=fin, tile_position=(0, 0),
                                             skip_group_check=True)
                                    t.matmul(pvp[64:128, :], vv[:, 128 * sc + 64:128 * sc + 128],
                                             eB[:, 512 * li:512 * (li + 1)],
                                             start=st0, stop=fin, tile_position=(0, 64),
                                             skip_group_check=True)
                                    t.matmul(dnp[0:64, :], onesb[:, :],
                                             eA[:, 512 * li:512 * (li + 1)],
                                             start=st0, stop=fin, tile_position=(0, 0),
                                             skip_group_check=True)
                                    mm = t.matmul(dnp[64:128, :], onesb[:, :],
                                                  eB[:, 512 * li:512 * (li + 1)],
                                                  start=st0, stop=fin, tile_position=(0, 64),
                                                  skip_group_check=True)
                                    if fin:
                                        mm.then_inc(S["pv"].h, 1); S["pv"].n += 1

                            def grp_masked(gi):
                                return [sc for sc in grps[gi]
                                        if sc >= P - 8 and MW[k][sc - (P - 8)] > 0]

                            def nmsk_through(gi):
                                n = msk_base[0]
                                for g2 in range(gi + 1):
                                    if grp_masked(g2):
                                        n += 2
                                return n

                            for gi in range(G):
                                gg = base + gi
                                grp = grps[gi]
                                for hh in range(2):
                                    if gg + hh >= 1:
                                        t.wait_ge(S["exp"].h, 2 * gg - 1 + hh)
                                    pss = psA if hh == 0 else psB
                                    for li, sc in enumerate(grp):
                                        mm = t.matmul(pss[:, 512 * li:512 * (li + 1)],
                                                      kw[64 * hh:64 * hh + 64, 128 * sc:128 * (sc + 1)],
                                                      qT[64 * hh:64 * hh + 64, qc0:qc0 + 512],
                                                      start=True, stop=True,
                                                      tile_position=(64 * hh, 0),
                                                      skip_group_check=True)
                                        if li == len(grp) - 1:
                                            mm.then_inc(S["tsc"].h, 1); S["tsc"].n += 1
                                if gi >= 1:
                                    if grp_masked(gi - 1):
                                        t.wait_ge(S["msk"].h, nmsk_through(gi - 1))
                                    pv_den(gi - 1, final=False)
                            t.wait_ge(S["exp"].h, 2 * (base + G))
                            if grp_masked(G - 1):
                                t.wait_ge(S["msk"].h, nmsk_through(G - 1))
                            pv_den(G - 1, final=True)
                            msk_base[0] = nmsk_through(G - 1)
                        for i in range(4):
                            for oh in range(2):
                                ch = 2 * i + oh
                                if ch == 0:
                                    t.wait_ge(S["div"].h, 8 * (k + 1))
                                if i >= 1:
                                    t.wait_ge(S["res"].h, 12 * k + 3 * (i - 1) + oh + 1)
                                pso = pvp if ch % 2 == 0 else dnp
                                for c in range(8):
                                    mm = t.matmul(pso[:, 0:512],
                                                  outT[:, 512 * c + 128 * i:512 * c + 128 * (i + 1)],
                                                  woutT_s[:, c * D + 512 * oh: c * D + 512 * (oh + 1)],
                                                  start=(c == 0), stop=(c == 7), skip_group_check=True)
                                mm.then_inc(S["op"].h, 1); S["op"].n += 1

                @blk.scalar
                def _(s):
                    for k in range(4):
                        G = NG[k]
                        grps = GRPS[k]
                        for j in range(8):
                            p = 8 * k + j
                            base = TSC_CUM[k] + j * G
                            for gi in range(G):
                                gg = base + gi
                                L = len(grps[gi])
                                s.wait_ge(S["tsc"].h, 2 * gg + 1)
                                s.activation(expb[gi % 2][0][:, 0:512 * L], psA[:, 0:512 * L],
                                             AF.Exp, scale=0.125).then_inc(S["exp"].h, 1); S["exp"].n += 1
                                s.wait_ge(S["tsc"].h, 2 * gg + 2)
                                s.activation(expb[gi % 2][1][:, 0:512 * L], psB[:, 0:512 * L],
                                             AF.Exp, scale=0.125).then_inc(S["exp"].h, 1); S["exp"].n += 1
                            s.wait_ge(S["pv"].h, p + 1)
                            s.activation(rcpa[:, :], dnp[:, :], AF.Ln)
                            s.activation(rcpb[:, :], rcpa[:, :], AF.Exp, scale=-1.0).then_inc(S["rcp"].h, 1); S["rcp"].n += 1
                        for i in range(4):
                            s.wait_ge(S["res"].h, 12 * k + 3 * i + 3)
                            s.activation(mv[:, 2:3], mv[:, 1:2], AF.Ln, bias=epsln[:, :])
                            s.drain()
                            s.activation(mv[:, 3:4], mv[:, 2:3], AF.Exp, scale=-0.5).then_inc(S["lnr"].h, 1); S["lnr"].n += 1

                @blk.vector
                def _(v):
                    for k in range(4):
                        P = SLOT_P[k]
                        grps = GRPS[k]
                        G = NG[k]
                        for j in range(8):
                            p = 8 * k + j
                            base = TSC_CUM[k] + j * G
                            for gi in range(G):
                                gg = base + gi
                                grp = grps[gi]
                                msk = [(li, sc) for li, sc in enumerate(grp)
                                       if sc >= P - 8 and MW[k][sc - (P - 8)] > 0]
                                if not msk:
                                    continue
                                for hh in range(2):
                                    v.wait_ge(S["exp"].h, 2 * gg + 1 + hh)
                                    eT = expb[gi % 2][hh]
                                    for li, sc in msk:
                                        r = sc - (P - 8)
                                        w = MW[k][r]
                                        mi = v.tensor_tensor(eT[:, 512 * li:512 * li + w],
                                                             eT[:, 512 * li:512 * li + w],
                                                             mskb[k % 2][:, 512 * r:512 * r + w],
                                                             ALU.mult)
                                    mi.then_inc(S["msk"].h, 1); S["msk"].n += 1
                            if j == 0:
                                v.wait_ge(S["op"].h, 8 * k)
                            v.wait_ge(S["rcp"].h, p + 1)
                            v.tensor_tensor(outT[:, 512 * j:512 * (j + 1)], pvp[:, :], rcpb[:, :],
                                            ALU.mult).then_inc(S["div"].h, 1); S["div"].n += 1
                            while wqst["mq"] < min(64, 2 * (p + 1)):
                                if wqst["mq"] < 32:
                                    emit_w1_quant(v)
                                else:
                                    emit_w2_group(v)
                        for i in range(4):
                            for oh in range(2):
                                v.wait_ge(S["op"].h, 8 * k + 2 * i + oh + 1)
                                v.wait_ge(S["xrl"].h, 16 * (4 * k + i + 1))
                                pso = pvp if (2 * i + oh) % 2 == 0 else dnp
                                v.tensor_tensor(xr[:, 512 * oh:512 * (oh + 1)], pso[:, 0:512],
                                                xres[:, 512 * oh:512 * (oh + 1)], ALU.add).then_inc(S["res"].h, 1); S["res"].n += 1
                            v.bn_stats(bns[:, 0:6], xr[:, 0:512])
                            v.bn_stats(bns[:, 6:12], xr[:, 512:1024])
                            v.drain()
                            v.bn_aggr(mv[:, 0:2], bns[:, :]).then_inc(S["res"].h, 1); S["res"].n += 1
                            v.wait_ge(S["lnr"].h, 4 * k + i + 1)
                            v.tensor_tensor(mv[:, 2:3], mv[:, 0:1], mv[:, 3:4], ALU.mult)
                            v.drain()
                            ti = 4 * k + i
                            if ti >= 2:
                                v.wait_ge(S["xqt"].h, 32 * (ti - 1))
                            v.tensor_scalar(x2stg[ti % 2][:, :], xr[:, :], mv[:, 3:4], mv[:, 2:3],
                                            ALU.mult, ALU.subtract)
                            v.reduce_max(x2max[:, ti:ti + 1], x2stg[ti % 2][:, :],
                                         axis=AX.X).then_inc(S["xt2"].h, 1); S["xt2"].n += 1

        pw2.close()

        with ExitStack() as p3:
            E3 = p3.enter_context
            xqT = x2T
            q32 = E3(nc.sbuf_tensor("q32", [128, D], F32))
            qi32 = E3(nc.sbuf_tensor("qi32", [128, D], I32))
            sS = E3(nc.sbuf_tensor("sS", [128, 32], F32))
            sSi = E3(nc.sbuf_tensor("sSi", [128, 32], F32))
            q32w = E3(nc.sbuf_tensor("q32w", [128, 8, 512], F32))
            qi32w = E3(nc.sbuf_tensor("qi32w", [128, 8, 512], I32))
            w1blk = [E3(nc.sbuf_tensor(f"w1blk{i}", [128, 1024], BF16)) for i in range(2)]
            hsb = [E3(nc.sbuf_tensor(f"hsb{i}", [128, TQ], BF16)) for i in range(2)]
            f1ps = [E3(nc.psum_tensor(f"f1ps{i}", [128, 512], F32)) for i in range(8)]

            with (nc.Block() if phases >= 3 else _NullBlock()) as blk:
                if phases >= 4:
                    @blk.gpsimd
                    def _(g):
                        for jf in range(2):
                            g.dma_start(out=w1blk[jf][:, :].rearrange("p (a f) -> p a f", a=8),
                                        in_=w1qt_d.ap()[jf, :, :, :]).then_inc(S["w1l"].h, 16); S["w1l"].n += 16

                @blk.vector
                def _(v):
                    v.drain()
                    v.reduce_max(red1[:, 0:1], x2max[:, :], axis=AX.X)
                    v.drain()
                    v.nop().then_inc(S["v3"].h, 1); S["v3"].n += 1

                @blk.gpsimd
                def _(g):
                    g.wait_ge(S["v3"].h, 1)
                    g.dma_start(out=bass.AP(b2row_d, 0, [[1, 128]]), in_=red1[:, 0:1]).then_inc(S["gp"].h, 16); S["gp"].n += 16
                    g.wait_ge(S["gp"].h, S["gp"].n)
                    g.dma_start(out=q32[0:1, 0:128], in_=bass.AP(b2row_d, 0, [[128, 1], [1, 128]])).then_inc(S["gp"].h, 16); S["gp"].n += 16
                    g.wait_ge(S["gp"].h, S["gp"].n)
                    g.nop().then_inc(S["gp"].h, 1); S["gp"].n += 1

                @blk.vector
                def _(v):
                    v.wait_ge(S["gp"].h, S["gp"].n)
                    v.reduce_max(tiny[0:1, 0:1], q32[0:1, 0:128], axis=AX.X)
                    v.drain()
                    v.nop().then_inc(S["v3"].h, 1); S["v3"].n += 1

                @blk.gpsimd
                def _(g):
                    g.wait_ge(S["v3"].h, 2)
                    g.dma_start(out=cc_in.ap(), in_=tiny[0:1, :]).then_inc(S["gp"].h, 16); S["gp"].n += 16
                    g.wait_ge(S["gp"].h, S["gp"].n)
                    g.collective_compute("AllReduce", ALU.max, replica_groups=[list(range(NC))],
                                         ins=[cc_in.ap().opt()], outs=[cc_out.ap().opt()]).then_inc(S["cc"].h, 1); S["cc"].n += 1
                    g.wait_ge(S["cc"].h, S["cc"].n)
                    g.dma_start(out=tiny2[0:1, :], in_=cc_out.ap()).then_inc(S["gp"].h, 16); S["gp"].n += 16

                @blk.vector
                def _(v):
                    v.wait_ge(S["gp"].h, S["gp"].n)
                    v.tensor_scalar(tiny[0:1, 0:1], tiny2[0:1, 0:1], EPS_Q, 1.0 / 255.0, ALU.max, ALU.mult)
                    v.drain()
                    v.reciprocal(tiny[0:1, 1:2], tiny[0:1, 0:1])
                    v.drain()
                    v.nop().then_inc(S["v3"].h, 1); S["v3"].n += 1

                @blk.gpsimd
                def _(g):
                    g.wait_ge(S["v3"].h, 3)
                    g.dma_start(out=bc_d.ap(), in_=tiny[0:1, :]).then_inc(S["gp"].h, 16); S["gp"].n += 16
                    g.wait_ge(S["gp"].h, S["gp"].n)
                    g.dma_start(out=scal8[:, :], in_=bass.AP(bc_d, 0, [[0, 128], [1, 8]])).then_inc(S["gp"].h, 16); S["gp"].n += 16

                @blk.vector
                def _(v):
                    v.wait_ge(S["gp"].h, S["gp"].n)
                    v.tensor_scalar(sS[:, :], sw1[:, :], scal8[:, 0:1], None, ALU.mult)
                    v.drain()
                    v.reciprocal(sSi[:, :], sS[:, :])
                    v.drain()
                    v.tensor_tensor(q32[:, 0:32], b1s[:, :], sSi[:, :], ALU.mult)
                    v.drain()
                    v.tensor_copy(qi32[:, 0:32], q32[:, 0:32])
                    v.drain()
                    v.tensor_tensor(b1qs[:, :], qi32[:, 0:32], sS[:, :], ALU.mult).then_inc(S["v3"].h, 1); S["v3"].n += 1
                    for w in range(4):
                        if w == 3:
                            v.wait_ge(S["xqt"].h, 32 * 16)
                        src_ap = x2T[:, :, 512 * w:512 * (w + 1)]
                        v.tensor_scalar(q32w[:, :, :], src_ap, scal8[:, 1:2], 0.0, ALU.mult, ALU.max)
                        v.tensor_copy(qi32w[:, :, :], q32w[:, :, :])
                        v.tensor_scalar(src_ap, qi32w[:, :, :], scal8[:, 0:1], None,
                                        ALU.mult).then_inc(S["xql"].h, 1); S["xql"].n += 1

                @blk.sync
                def _(sy):
                    ti = 15
                    sy.wait_ge(S["xt2"].h, ti + 1)
                    sy.dma_start_transpose(x2T[:, :, 128 * ti:128 * (ti + 1)],
                                           x2stg[ti % 2][:, :]).then_inc(S["xqt"].h, 16); S["xqt"].n += 16
                    sy.dma_start(out=x2_d.ap()[128 * ti:128 * (ti + 1), :],
                                 in_=x2stg[ti % 2][:, :]).then_inc(S["xqt"].h, 16); S["xqt"].n += 16

                if phases >= 4:
                    @blk.gpsimd
                    def _(g):
                        for jf in range(2, 32):
                            g.wait_ge(S["f1"].h, 4 * (jf - 1))
                            g.dma_start(out=w1blk[jf % 2][:, :].rearrange("p (a f) -> p a f", a=8),
                                        in_=w1qt_d.ap()[jf, :, :, :]).then_inc(S["w1l"].h, 16); S["w1l"].n += 16

                    @blk.tensor
                    def _(t):
                        for jf in range(32):
                            t.wait_ge(S["w1l"].h, 16 * (jf + 1))
                            if jf >= 2:
                                t.wait_ge(S["hev"].h, 4 * (jf - 1))
                            bs = 4 * (jf % 2)
                            for w in range(4):
                                if jf == 0:
                                    t.wait_ge(S["xql"].h, w + 1)
                                for c in range(8):
                                    mm = t.matmul(f1ps[bs + w][:, :],
                                                  w1blk[jf % 2][:, 128 * c:128 * (c + 1)],
                                                  xqT[:, c:c + 1, 512 * w:512 * (w + 1)],
                                                  start=(c == 0), stop=(c == 7))
                                mm.then_inc(S["f1"].h, 1); S["f1"].n += 1

                    @blk.vector
                    def _(v):
                        for jf in range(32):
                            if jf >= 2:
                                v.wait_ge(S["hsp"].h, 16 * (jf - 1))
                            bs = 4 * (jf % 2)
                            for w in range(4):
                                nch = 4 * jf + w
                                v.wait_ge(S["f1"].h, 4 * jf + w + 1)
                                v.tensor_scalar(hsb[jf % 2][:, 512 * w:512 * (w + 1)], f1ps[bs + w][:, :],
                                                b1qs[:, jf:jf + 1], NEG, ALU.add, ALU.max,
                                                accum_out=hstat[:, nch:nch + 1]).then_inc(S["hev"].h, 1); S["hev"].n += 1

                    @blk.sync
                    def _(sy):
                        for jf in range(32):
                            sy.wait_ge(S["hev"].h, 4 * (jf + 1))
                            sy.dma_start(out=h_d.ap()[jf, :, :], in_=hsb[jf % 2][:, :]).then_inc(S["hsp"].h, 16); S["hsp"].n += 16

            with (nc.Block() if phases >= 5 else _NullBlock()) as blk:
                @blk.vector
                def _(v):
                    v.reduce_max(red1[:, 0:1], hstat[:, :], axis=AX.X)
                    v.drain()
                    v.nop().then_inc(S["v5"].h, 1); S["v5"].n += 1

                @blk.gpsimd
                def _(g):
                    g.wait_ge(S["v5"].h, 1)
                    g.dma_start(out=bass.AP(b2row_d, 0, [[1, 128]]), in_=red1[:, 0:1]).then_inc(S["gp"].h, 16); S["gp"].n += 16
                    g.wait_ge(S["gp"].h, S["gp"].n)
                    g.dma_start(out=q32[0:1, 0:128], in_=bass.AP(b2row_d, 0, [[128, 1], [1, 128]])).then_inc(S["gp"].h, 16); S["gp"].n += 16
                    g.wait_ge(S["gp"].h, S["gp"].n)
                    g.nop().then_inc(S["gp"].h, 1); S["gp"].n += 1

                @blk.vector
                def _(v):
                    v.wait_ge(S["gp"].h, S["gp"].n)
                    v.reduce_max(tiny[0:1, 0:1], q32[0:1, 0:128], axis=AX.X)
                    v.drain()
                    v.nop().then_inc(S["v5"].h, 1); S["v5"].n += 1

                @blk.gpsimd
                def _(g):
                    g.wait_ge(S["v5"].h, 2)
                    g.dma_start(out=cc_in.ap(), in_=tiny[0:1, :]).then_inc(S["gp"].h, 16); S["gp"].n += 16
                    g.wait_ge(S["gp"].h, S["gp"].n)
                    g.collective_compute("AllReduce", ALU.max, replica_groups=[list(range(NC))],
                                         ins=[cc_in.ap().opt()], outs=[cc_out.ap().opt()]).then_inc(S["cc"].h, 1); S["cc"].n += 1
                    g.wait_ge(S["cc"].h, S["cc"].n)
                    g.dma_start(out=tiny2[0:1, :], in_=cc_out.ap()).then_inc(S["gp"].h, 16); S["gp"].n += 16

                @blk.vector
                def _(v):
                    v.wait_ge(S["gp"].h, S["gp"].n)
                    v.tensor_scalar(tiny[0:1, 2:3], tiny2[0:1, 0:1], EPS_Q, 1.0 / 255.0, ALU.max, ALU.mult)
                    v.drain()
                    v.reciprocal(tiny[0:1, 3:4], tiny[0:1, 2:3])
                    v.drain()
                    v.nop().then_inc(S["v5"].h, 1); S["v5"].n += 1

                @blk.gpsimd
                def _(g):
                    g.wait_ge(S["v5"].h, 3)
                    g.dma_start(out=bc_d.ap(), in_=tiny[0:1, :]).then_inc(S["gp"].h, 16); S["gp"].n += 16
                    g.wait_ge(S["gp"].h, S["gp"].n)
                    g.dma_start(out=scal8[:, :], in_=bass.AP(bc_d, 0, [[0, 128], [1, 8]])).then_inc(S["gp"].h, 16); S["gp"].n += 16

                @blk.vector
                def _(v):
                    v.wait_ge(S["gp"].h, S["gp"].n)
                    v.tensor_scalar(sS[:, 0:8], sw2[:, :], scal8[:, 2:3], None, ALU.mult)
                    v.drain()
                    v.reciprocal(sSi[:, 0:8], sS[:, 0:8])
                    v.drain()
                    v.tensor_tensor(q32[:, 0:8], b2s[:, :], sSi[:, 0:8], ALU.mult)
                    v.drain()
                    v.tensor_copy(qi32[:, 0:8], q32[:, 0:8])
                    v.drain()
                    v.tensor_tensor(b2qs[:, :], qi32[:, 0:8], sS[:, 0:8], ALU.mult).then_inc(S["v5"].h, 1); S["v5"].n += 1

                @blk.gpsimd
                def _(g):
                    g.wait_ge(S["v5"].h, 4)
                    g.dma_start(out=bass.AP(b2row_d, 0, [[1, 128], [128, 8]]),
                                in_=b2qs[:, :], allow_slow_non_contiguous=True).then_inc(S["gp"].h, 16); S["gp"].n += 16
                    g.wait_ge(S["gp"].h, S["gp"].n)
                    g.dma_start(out=b2qb[:, :], in_=bass.AP(b2row_d, 0, [[0, 128], [1, D]])).then_inc(S["gp"].h, 16); S["gp"].n += 16
                    g.wait_ge(S["gp"].h, S["gp"].n)
                    if debug:
                        for nm, src in [("b2qb", b2qb[:, :]), ("b1q", b1qs[:, :]),
                                        ("hstat", hstat[:, :]), ("sw1", sw1[:, :]), ("sw2", sw2[:, :]),
                                        ("s12", bc_d.ap()),
                                        ("xqT", xqT[:, :, :].rearrange("p a f -> p (a f)"))]:
                            g.dma_start(out=dbg[nm].ap(), in_=src).then_inc(S["gp"].h, 16); S["gp"].n += 16
                        g.wait_ge(S["gp"].h, S["gp"].n)

        with ExitStack() as p6:
            E6 = p6.enter_context
            hload = [E6(nc.sbuf_tensor(f"hload{i}", [128, 1024], BF16)) for i in range(4)]
            x2ld = [E6(nc.sbuf_tensor(f"x2ld{i}", [128, D], BF16)) for i in range(2)]
            hq = E6(nc.sbuf_tensor("hq", [128, 32 * 1024], BF16))
            w2blk = [E6(nc.sbuf_tensor(f"w2blk{i}", [128, 512], BF16)) for i in range(4)]
            y2sb = E6(nc.sbuf_tensor("y2sb", [128, 8 * D], F32))
            q32b = E6(nc.sbuf_tensor("q32b", [128, 1024], F32))
            qi32b = E6(nc.sbuf_tensor("qi32b", [128, 1024], I32))
            xr6 = E6(nc.sbuf_tensor("xr6", [128, D], F32))
            outt = [E6(nc.sbuf_tensor(f"outt{i}", [128, D], F32)) for i in range(2)]
            bns6 = E6(nc.sbuf_tensor("bns6", [128, 12], F32))
            mv6 = E6(nc.sbuf_tensor("mv6", [128, 4], F32))
            y2ps = [E6(nc.psum_tensor(f"y2ps{i}", [128, 512], F32)) for i in range(8)]

            EVB = [0, 8, 24, 32]

            with (nc.Block() if phases >= 6 else _NullBlock()) as blk:
                @blk.gpsimd
                def _(g):
                    for n in range(64):
                        if n >= 4:
                            g.wait_ge(S["hq"].h, n - 3)
                        tb, jf = n // 32, n % 32
                        g.dma_start(out=hload[n % 4][:, :],
                                    in_=h_d.ap()[jf, :, 1024 * tb:1024 * (tb + 1)]).then_inc(S["hl"].h, 16); S["hl"].n += 16
                    if debug:
                        for nm, src in [("qT", qT[:, :]),
                                        ("kt0", kt_d.ap()[0, :, :]), ("v0", v_d.ap()[0, :, :]),
                                        ("h0", h_d.ap()[0, :, :])]:
                            g.dma_start(out=dbg[nm].ap(), in_=src).then_inc(S["gp"].h, 16); S["gp"].n += 16
                        g.dma_start(out=dbg["x2"].ap().rearrange("p (a d) -> p a d", a=16),
                                    in_=x2_d.ap().rearrange("(a p) d -> p a d", p=128)).then_inc(S["gp"].h, 16); S["gp"].n += 16
                        g.wait_ge(S["gp"].h, S["gp"].n)

                @blk.sync
                def _(sy):
                    for tb in range(2):
                        for oh in range(2):
                            for jf in range(32):
                                n = 64 * tb + 32 * oh + jf
                                if n >= 4:
                                    sy.wait_ge(S["f2"].h, n - 3)
                                sy.dma_start(out=w2blk[n % 4][:, :],
                                             in_=w2qt_d.ap()[jf, :, 512 * oh:512 * (oh + 1)]).then_inc(S["w2l"].h, 16); S["w2l"].n += 16
                        for i in range(8):
                            ti = 8 * tb + i
                            sy.wait_ge(S["ot"].h, ti + 1)
                            sy.dma_start(out=out.ap()[128 * ti:128 * (ti + 1), :],
                                         in_=outt[ti % 2][:, :]).then_inc(S["o"].h, 16); S["o"].n += 16
                    sy.wait_ge(S["o"].h, S["o"].n)

                @blk.vector
                def _(v):
                    def hquant(n):
                        jf = n % 32
                        v.wait_ge(S["hl"].h, 16 * (n + 1))
                        if n >= 32:
                            v.wait_ge(S["f2"].h, 32 + jf + 1)
                        v.tensor_scalar(q32b[:, :], hload[n % 4][:, :], scal8[:, 3:4], 0.0, ALU.mult, ALU.max)
                        v.tensor_copy(qi32b[:, :], q32b[:, :])
                        v.tensor_scalar(hq[:, 1024 * jf:1024 * (jf + 1)], qi32b[:, :], 255.0, scal8[:, 2:3],
                                        ALU.min, ALU.mult).then_inc(S["hq"].h, 1); S["hq"].n += 1

                    for tb in range(2):
                        if tb == 0:
                            for n in range(32):
                                hquant(n)
                        for oh in range(2):
                            if tb == 0 and oh == 1:
                                for n in range(32, 56):
                                    hquant(n)
                            v.wait_ge(S["f2"].h, 32 * (2 * tb + oh + 1))
                            for i in range(8):
                                v.tensor_tensor(y2sb[:, D * i + 512 * oh:D * i + 512 * (oh + 1)],
                                                y2ps[i][:, :], b2qb[:, 512 * oh:512 * (oh + 1)],
                                                ALU.add).then_inc(S["y2e"].h, 1); S["y2e"].n += 1
                            if tb == 0 and oh == 1:
                                for n in range(56, 64):
                                    hquant(n)
                        for i in range(8):
                            ti = 8 * tb + i
                            v.wait_ge(S["x2l"].h, 16 * (ti + 1))
                            v.tensor_tensor(xr6[:, :], y2sb[:, D * i:D * (i + 1)],
                                            x2ld[ti % 2][:, :], ALU.add)
                            v.bn_stats(bns6[:, 0:6], xr6[:, 0:512])
                            v.bn_stats(bns6[:, 6:12], xr6[:, 512:1024])
                            v.drain()
                            v.bn_aggr(mv6[:, 0:2], bns6[:, :]).then_inc(S["y2e"].h, 1); S["y2e"].n += 1
                            v.wait_ge(S["ln2"].h, ti + 1)
                            v.tensor_tensor(mv6[:, 2:3], mv6[:, 0:1], mv6[:, 3:4], ALU.mult)
                            v.drain()
                            if ti >= 2:
                                v.wait_ge(S["o"].h, 16 * (ti - 1))
                            v.tensor_scalar(outt[ti % 2][:, :], xr6[:, :], mv6[:, 3:4], mv6[:, 2:3],
                                            ALU.mult, ALU.subtract).then_inc(S["ot"].h, 1); S["ot"].n += 1

                @blk.tensor
                def _(t):
                    for tb in range(2):
                        for oh in range(2):
                            sec = 2 * tb + oh
                            for jf in range(32):
                                n = 64 * tb + 32 * oh + jf
                                t.wait_ge(S["hq"].h, 32 * tb + jf + 1)
                                t.wait_ge(S["w2l"].h, 16 * (n + 1))
                                for i in range(8):
                                    if jf == 0 and sec >= 1:
                                        t.wait_ge(S["y2e"].h, EVB[sec - 1] + i + 1)
                                    mm = t.matmul(y2ps[i][:, :],
                                                  hq[:, 1024 * jf + 128 * i:1024 * jf + 128 * (i + 1)],
                                                  w2blk[n % 4][:, :],
                                                  start=(jf == 0), stop=(jf == 31), skip_group_check=True)
                                mm.then_inc(S["f2"].h, 1); S["f2"].n += 1

                @blk.scalar
                def _(s):
                    for ti in range(2):
                        s.dma_start(out=x2ld[ti % 2][:, :],
                                    in_=x2_d.ap()[128 * ti:128 * (ti + 1), :]).then_inc(S["x2l"].h, 16); S["x2l"].n += 16
                    for tb in range(2):
                        for i in range(8):
                            ti = 8 * tb + i
                            s.wait_ge(S["y2e"].h, 24 * tb + 16 + i + 1)
                            s.activation(mv6[:, 2:3], mv6[:, 1:2], AF.Ln, bias=epsln[:, :])
                            s.drain()
                            s.activation(mv6[:, 3:4], mv6[:, 2:3], AF.Exp, scale=-0.5).then_inc(S["ln2"].h, 1); S["ln2"].n += 1
                            if ti + 2 < 16:
                                s.wait_ge(S["ot"].h, ti + 1)
                                s.dma_start(out=x2ld[ti % 2][:, :],
                                            in_=x2_d.ap()[128 * (ti + 2):128 * (ti + 3), :]).then_inc(S["x2l"].h, 16); S["x2l"].n += 16

        if debug and phases < 6:
            with nc.Block() as blk:
                @blk.gpsimd
                def _(g):
                    pairs = [("kt0", kt_d.ap()[0, :, :]), ("v0", v_d.ap()[0, :, :]),
                             ("qT", qT[:, :])]
                    pass
                    if phases >= 3:
                        pairs.append(("s12", bc_d.ap()))
                        pairs.append(("b1q", b1qs[:, :]))
                        pairs.append(("sw1", sw1[:, :]))
                    if phases >= 4:
                        pairs.append(("h0", h_d.ap()[0, :, :]))
                        pairs.append(("hstat", hstat[:, :]))
                    for nm, srcp in pairs:
                        g.dma_start(out=dbg[nm].ap(), in_=srcp).then_inc(S["gp"].h, 16); S["gp"].n += 16
                    g.wait_ge(S["gp"].h, S["gp"].n)

    return nc, dbg


def _host_masks(parity):
    m = np.zeros((4, 8, 128, 512), dtype=np.float32)
    for k in range(4):
        P = SLOT_P[k]
        T0 = 512 * QCH[parity][k]
        for r in range(8):
            S0 = 128 * (P - 8 + r)
            srow = S0 + np.arange(128)[:, None]
            tcol = T0 + np.arange(512)[None, :]
            m[k, r] = np.where(srow > tcol, 0.0, 1.0)
    return np.ascontiguousarray(m.astype(ml_dtypes.bfloat16))


_CACHE = {}


def kernel(L0_attn_x, L0_attn_W_qkv, L0_attn_W_out, L0_ffn_w1, L0_ffn_b1, L0_ffn_w2, L0_ffn_b2,
           debug=False):
    x = np.ascontiguousarray(np.asarray(L0_attn_x, dtype=np.float32))
    wqkvT_h = np.ascontiguousarray(np.asarray(L0_attn_W_qkv, dtype=np.float32).T)
    woutT_h = np.ascontiguousarray(np.asarray(L0_attn_W_out, dtype=np.float32).T)
    w1_h = np.ascontiguousarray(np.asarray(L0_ffn_w1, dtype=np.float32))
    b1_h = np.ascontiguousarray(np.asarray(L0_ffn_b1, dtype=np.float32).reshape(32, 128).T)
    w2_h = np.ascontiguousarray(np.asarray(L0_ffn_w2, dtype=np.float32))
    b2_h = np.ascontiguousarray(np.asarray(L0_ffn_b2, dtype=np.float32).reshape(8, 128).T)
    ident = np.ascontiguousarray(np.eye(128, dtype=np.float32).astype(ml_dtypes.bfloat16))

    key = ("prog", bool(debug), PHASES)
    if key not in _CACHE:
        _CACHE[key] = build_program(debug=bool(debug))
    nc, _ = _CACHE[key]

    in_maps = []
    for c in range(NC):
        b, parity = c // 2, c % 2
        xb = np.ascontiguousarray(x[b])
        rows = np.ascontiguousarray(
            np.concatenate([xb[512 * cq:512 * (cq + 1)] for cq in QCH[parity]], axis=0))
        in_maps.append({
            "x_kv": xb, "x_own": rows,
            "wqkvT": wqkvT_h, "woutT": woutT_h,
            "w1": w1_h, "b1": b1_h, "w2": w2_h, "b2": b2_h,
            "masks": _host_masks(parity), "ident_bf": ident,
        })
    res = run_bass_kernel_spmd(nc, in_maps, core_ids=list(range(NC)))
    outp = np.empty((B, T, D), dtype=np.float32)
    for c in range(NC):
        b, parity = c // 2, c % 2
        o = res.results[c]["out"]
        for k, cq in enumerate(QCH[parity]):
            outp[b, 512 * cq:512 * (cq + 1)] = o[512 * k:512 * (k + 1)]
    if debug:
        return outp, res
    return outp
